# version 29
# baseline (speedup 1.0000x reference)
import numpy as np
import ml_dtypes
from contextlib import ExitStack

import concourse.bass as bass
import concourse.mybir as mybir
from concourse.bass_utils import run_bass_kernel_spmd

F32 = mybir.dt.float32
BF16 = mybir.dt.bfloat16
I16 = mybir.dt.int16
I32 = mybir.dt.int32
ALU = mybir.AluOpType
AF = mybir.ActivationFunctionType
AX = mybir.AxisListType

B, S, H = 2, 8192, 1024
NCORES = 8
T = (B * S) // NCORES
TS = 512
TCN = T // TS
HC = H // 128
I_FFN = 2816
IC = I_FFN // 128
D_IN = 2048
DC = D_IN // 128
NST = 16
CHAIN = 4
QMAX = 127.0

ENGINES = ("sync", "scalar", "vector", "gpsimd", "tensor")

_TAPS = ()


class PK:

    DMA_SLOTS = 6

    def __init__(self, nc, es):
        self.nc = nc
        self.q = {e: [] for e in ENGINES}
        self.esem = {e: es.enter_context(nc.semaphore(f"es_{e}")) for e in ENGINES}
        self.dsem = {e: [es.enter_context(nc.semaphore(f"ds_{e}{i}"))
                         for i in range(self.DMA_SLOTS)] for e in ENGINES}
        self.ecnt = {e: 0 for e in ENGINES}
        self.dcnt = {e: [0] * self.DMA_SLOTS for e in ENGINES}
        self.dslot = {e: 0 for e in ENGINES}
        self.reg = {}
        self.waited = {e: {} for e in ENGINES}

    def _r(self, key):
        if key not in self.reg:
            self.reg[key] = {"w": None, "r": []}
        return self.reg[key]

    def _wants(self, eng, reads, writes):
        toks = []
        for k in reads:
            r = self._r(k)
            if r["w"] is not None:
                toks.append(r["w"])
        for k in writes:
            r = self._r(k)
            if r["w"] is not None:
                toks.append(r["w"])
            toks.extend(r["r"])
        best = {}
        for sem, v in toks:
            if best.get(sem.name, (None, -1))[1] < v:
                best[sem.name] = (sem, v)
        out = []
        for sem, v in best.values():
            if self.waited[eng].get(sem.name, -1) < v:
                out.append((sem, v))
                self.waited[eng][sem.name] = v
        return out

    def _update(self, tok, reads, writes):
        for k in reads:
            self._r(k)["r"].append(tok)
        for k in writes:
            r = self._r(k)
            r["w"] = tok
            r["r"] = []

    def _fresh_mem_barrier(self, eng, writes):
        fresh = [k for k in writes if k not in self.reg]
        if fresh and "__mem__" in self.reg:
            return self._wants(eng, (), ("__mem__",))
        return []

    def note_freed(self, keys):
        m = self._r("__mem__")
        for k in keys:
            r = self.reg.get(k)
            if r is None:
                continue
            if r["w"] is not None:
                m["r"].append(r["w"])
            m["r"].extend(r["r"])

    def op(self, eng, fn, reads=(), writes=(), inc=True):
        waits = self._fresh_mem_barrier(eng, writes) + self._wants(eng, reads, writes)
        sem = self.esem[eng]
        if inc:
            self.ecnt[eng] += 1
            tok = (sem, self.ecnt[eng])
            self._update(tok, reads, writes)
        val = self.ecnt[eng]

        def emit(e, fn=fn, waits=waits, inc=inc, sem=sem, val=val):
            for s, v in waits:
                e.wait_ge(s, v)
            inst = fn(e)
            if inc:
                inst.then_inc(sem, 1)
        self.q[eng].append(emit)

    def _dma_slot(self, eng):
        slot = self.dslot[eng]
        self.dslot[eng] = (slot + 1) % self.DMA_SLOTS
        sem = self.dsem[eng][slot]
        prev = self.dcnt[eng][slot]
        self.dcnt[eng][slot] = prev + 16
        extra = []
        if prev > 0 and self.waited[eng].get(sem.name, -1) < prev:
            extra.append((sem, prev))
            self.waited[eng][sem.name] = prev
        return sem, prev + 16, extra

    def dma(self, eng, out_ap, in_ap, reads=(), writes=()):
        waits = self._fresh_mem_barrier(eng, writes) + self._wants(eng, reads, writes)
        sem, val, extra = self._dma_slot(eng)
        waits = extra + waits
        tok = (sem, val)
        self._update(tok, reads, writes)

        def emit(e, out_ap=out_ap, in_ap=in_ap, waits=waits, sem=sem):
            for s, v in waits:
                e.wait_ge(s, v)
            oa = out_ap() if callable(out_ap) else out_ap
            ia = in_ap() if callable(in_ap) else in_ap
            e.dma_start(out=oa, in_=ia).then_inc(sem, 16)
        self.q[eng].append(emit)

    def dyn_dma(self, eng, fn, reads=(), writes=()):
        waits = self._fresh_mem_barrier(eng, writes) + self._wants(eng, reads, writes)
        sem, val, extra = self._dma_slot(eng)
        waits = extra + waits
        tok = (sem, val)
        self._update(tok, reads, writes)

        def emit(e, fn=fn, waits=waits, sem=sem):
            for s, v in waits:
                e.wait_ge(s, v)
            fn(e).then_inc(sem, 16)
        self.q[eng].append(emit)

    def raw(self, eng, fn):
        self.q[eng].append(lambda e, fn=fn: fn(e))

    def barrier_wait(self, eng, keys):
        waits = self._wants(eng, keys, ())

        def emit(e, waits=waits):
            for s, v in waits:
                e.wait_ge(s, v)
        self.q[eng].append(emit)

    def finalize(self, block):
        for e in ENGINES:
            getattr(block, e)(lambda eng, q=tuple(self.q[e]): [f(eng) for f in q])


class Mem:
    def __init__(self, nc):
        self.nc = nc
        self.guards = {}
        self.seq = 0

    def sb(self, name, shape, dtype, side="left"):
        g = self.nc.sbuf_tensor(name, shape, dtype, side=side)
        h = g.__enter__()
        self.seq += 1
        self.guards[name] = (g, self.seq)
        return h

    def ps(self, name, shape, dtype):
        g = self.nc.psum_tensor(shape, dtype)
        h = g.__enter__()
        self.seq += 1
        self.guards[name] = (g, self.seq)
        return h

    def free(self, *names, P=None, keys=()):
        if P is not None:
            P.note_freed(keys if keys else names)
        for n in sorted(names, key=lambda n: -self.guards[n][1]):
            self.guards.pop(n)[0].__exit__(None, None, None)


def build_program(taps=()):
    nc = bass.Bass("TRN2", target_bir_lowering=False)
    es = ExitStack()

    d_x = nc.dram_tensor("x_t", [H, T], F32, kind="ExternalInput").ap()
    d_win = nc.dram_tensor("w_in_t", [2 * DC, 128, HC * 128], F32, kind="ExternalInput").ap()
    d_wo = nc.dram_tensor("w_out_t", [HC, 128, DC * 128], F32, kind="ExternalInput").ap()
    d_wg = nc.dram_tensor("w_gate", [I_FFN, H], F32, kind="ExternalInput").ap()
    d_wu = nc.dram_tensor("w_up", [I_FFN, H], F32, kind="ExternalInput").ap()
    d_wd = nc.dram_tensor("w_down", [H, I_FFN], F32, kind="ExternalInput").ap()
    d_xpw = nc.dram_tensor("x_proj_w", [D_IN, 48], F32, kind="ExternalInput").ap()
    d_dtw = nc.dram_tensor("dt_proj_w", [NST, D_IN], F32, kind="ExternalInput").ap()
    d_bin = nc.dram_tensor("b_in_t", [128, 2 * DC], F32, kind="ExternalInput").ap()
    d_bo = nc.dram_tensor("b_out_t", [128, HC], F32, kind="ExternalInput").ap()
    d_xpb = nc.dram_tensor("x_proj_b3", [NST, 3], F32, kind="ExternalInput").ap()
    d_dtb = nc.dram_tensor("dt_b_t", [128, DC], F32, kind="ExternalInput").ap()
    d_cw = nc.dram_tensor("conv_w_t", [128, DC * 4], F32, kind="ExternalInput").ap()
    d_cb = nc.dram_tensor("conv_b_t", [128, DC], F32, kind="ExternalInput").ap()
    d_alog = nc.dram_tensor("a_log_t", [128, DC * NST], F32, kind="ExternalInput").ap()
    d_dsk = nc.dram_tensor("d_skip_t", [128, DC], F32, kind="ExternalInput").ap()
    d_clx = nc.dram_tensor("clip_x", [1, 1], F32, kind="ExternalInput").ap()
    d_cld = nc.dram_tensor("clip_down", [1, 1], F32, kind="ExternalInput").ap()
    d_eye = nc.dram_tensor("eye128", [128, 128], BF16, kind="ExternalInput").ap()
    d_ones1 = nc.dram_tensor("ones1x128", [1, 128], BF16, kind="ExternalInput").ap()
    d_ones1f = nc.dram_tensor("ones1f", [1, 128], F32, kind="ExternalInput").ap()
    d_rank = nc.dram_tensor("rank", [1, 1], I32, kind="ExternalInput").ap()
    d_msk = nc.dram_tensor("chain_masks", [128, 3], F32, kind="ExternalInput").ap()
    d_bsel = nc.dram_tensor("bnd_sel", [128, 8], F32, kind="ExternalInput").ap()

    d_y = nc.dram_tensor("y", [H, T], F32, kind="ExternalOutput").ap()

    s_wgT = nc.dram_tensor("s_wgT", [IC, 128, HC * 128], BF16, kind="Internal")
    s_wuT = nc.dram_tensor("s_wuT", [IC, 128, HC * 128], BF16, kind="Internal")
    s_wdT = nc.dram_tensor("s_wdT", [HC, 128, IC * 128], BF16, kind="Internal")
    s_x1 = nc.dram_tensor("s_x1", [128, HC * T], BF16, kind="Internal")
    s_xr = nc.dram_tensor("s_xr", [128, HC * T], BF16, kind="Internal")
    s_sres = nc.dram_tensor("s_sres", [128, DC * T], BF16, kind="Internal")
    s_xc = nc.dram_tensor("s_xc", [128, DC * T], BF16, kind="Internal")
    s_ab = nc.dram_tensor("s_ab", [128, IC * T], BF16, kind="Internal")
    s_bc = nc.dram_tensor("s_bc", [2, NST, T], BF16, kind="Internal")
    s_Bb = nc.dram_tensor("s_Bb", [NST, 2, 128, 1024], BF16, kind="Internal")
    s_Cb = nc.dram_tensor("s_Cb", [NST, 2, 128, 1024], BF16, kind="Internal")
    cin_mx = [nc.dram_tensor(f"cin_mx{f}", [1, 1], F32, kind="Internal").ap() for f in range(4)]
    cout_mx = [nc.dram_tensor(f"cout_mx{f}", [1, 1], F32, kind="Internal").ap() for f in range(4)]
    cin_bnd = nc.dram_tensor("cin_bnd", [128, DC * 3], BF16, kind="Internal").ap()
    cout_bnd = nc.dram_tensor("cout_bnd", [8, 128, DC * 3], BF16, kind="Internal").ap()
    cin_st = [nc.dram_tensor(f"cin_st{d}", [128, 2 * NST], F32, kind="Internal").ap() for d in range(DC)]
    cout_st = [nc.dram_tensor(f"cout_st{d}", [8, 128, 2 * NST], F32, kind="Internal") for d in range(DC)]
    _tapspec = {
        "z1": (HC * T, BF16), "xq1": (HC * T, BF16), "a1": (IC * T, BF16),
        "x1": (HC * T, BF16), "xin": (DC * T, BF16), "xc": (DC * T, BF16),
        "bcB": (T, BF16), "delta0": (T, F32), "h0": (T, BF16),
        "gated": (DC * T, BF16), "x2": (HC * T, BF16), "sres": (DC * T, BF16),
        "y0": (T, F32), "sloc0": (T, F32),
    }
    d_taps = {t: nc.dram_tensor(f"tap_{t}", [128, _tapspec[t][0]], _tapspec[t][1],
                                kind="ExternalOutput").ap() for t in taps}

    P = PK(nc, es)
    M = Mem(nc)
    RG = [list(range(NCORES))]

    onesc = M.sb("onesc", [128, 1], BF16)
    ones1 = M.sb("ones1", [1, 128], BF16)
    ones1f = M.sb("ones1f_sb", [1, 128], F32)
    eye = M.sb("eye", [128, 128], BF16)
    c_bin = M.sb("c_bin", [128, 2 * DC], F32)
    c_bo = M.sb("c_bo", [128, HC], F32)
    c_xpb3 = M.sb("c_xpb3", [NST, 3], F32)
    c_dtb = M.sb("c_dtb", [128, DC], F32)
    c_cw = M.sb("c_cw", [128, DC * 4], F32)
    c_cb = M.sb("c_cb", [128, DC], F32)
    c_dsk = M.sb("c_dsk", [128, DC], F32)
    acols = M.sb("acols", [128, DC * NST], F32)
    sgr = M.sb("sgr", [128, IC], F32)
    sur = M.sb("sur", [128, IC], F32)
    sdr = M.sb("sdr", [128, HC], F32)
    sig_x = M.sb("sig_x", [1, 1], F32)
    sig_h = M.sb("sig_h", [1, 1], F32)
    rankb = M.sb("rankb", [1, 1], I32)
    cmask = M.sb("cmask", [128, 3], F32)
    bselc = M.sb("bselc", [128, 8], F32)
    smtmp = M.sb("smtmp", [1, 4], F32)
    qcols = M.sb("qcols", [128, 16], F32)
    sgx = M.sb("sgx", [2, 128, IC], F32) if False else M.sb("sgx", [128, 2 * IC], F32)
    sux = M.sb("sux", [128, 2 * IC], F32)
    sdh = M.sb("sdh", [128, 2 * HC], F32)

    def col(t, j):
        return t[:, j:j + 1]

    for dst, src in ((c_bin, d_bin), (c_bo, d_bo), (c_xpb3, d_xpb), (c_dtb, d_dtb),
                     (c_cw, d_cw), (c_cb, d_cb), (c_dsk, d_dsk),
                     (eye, d_eye), (ones1, d_ones1), (ones1f, d_ones1f), (rankb, d_rank),
                     (cmask, d_msk), (bselc, d_bsel)):
        P.dma("sync", dst[:], src, writes=(dst.name,))
    P.dma("sync", acols[:], d_alog, writes=("acols",))
    P.op("vector", lambda e: e.memset(onesc[:], 1.0), writes=("onesc",))
    atmp = M.sb("atmp", [128, DC * NST], F32)
    P.op("scalar", lambda e: e.activation(out=atmp[:], in_=acols[:], func=AF.Exp),
         reads=("acols",), writes=("atmp",))
    P.op("vector", lambda e: e.tensor_scalar(out=acols[:], in0=atmp[:], scalar1=-1.0,
                                             scalar2=None, op0=ALU.mult),
         reads=("atmp",), writes=("acols",))
    M.free("atmp", P=P, keys=("atmp",))
    P.dma("sync", smtmp[:, 0:1], d_clx, writes=("smtmp0",))
    P.dma("sync", smtmp[:, 1:2], d_cld, writes=("smtmp1",))
    P.op("scalar", lambda e: e.activation(out=sig_x[:], in_=smtmp[:, 0:1], func=AF.Sigmoid),
         reads=("smtmp0",), writes=("sig_x",))
    P.op("scalar", lambda e: e.activation(out=sig_h[:], in_=smtmp[:, 1:2], func=AF.Sigmoid),
         reads=("smtmp1",), writes=("sig_h",))

    regs = {}

    def setup_regs(eng_name):
        def inner(g):
            for k in (1, 2, 3):
                rk = g.alloc_register(f"r_car{k}_{eng_name}")
                g.reg_load(rk, rankb[:1, :1])
                g.reg_add(rk, rk, 8 - k)
                g.reg_mod(rk, rk, 8)
                g.reg_mul(rk, rk, 128 * 2 * NST)
                regs[(eng_name, k)] = rk
            if eng_name == "gpsimd":
                r2 = g.alloc_register("r_bnd")
                g.reg_load(r2, rankb[:1, :1])
                g.reg_add(r2, r2, 7)
                g.reg_mod(r2, r2, 8)
                g.reg_mul(r2, r2, 128 * DC * 3)
                regs["bnd"] = r2
        return inner
    for _en in ("gpsimd", "scalar", "sync"):
        P.barrier_wait(_en, ("rankb",))
        P.raw(_en, setup_regs(_en))


    def quant_weight(src_ap, n_rows_tiles, in_dim, scale_buf, scratch, tag, base=0, q="sync"):
        in_c = in_dim // 128
        for rt in range(n_rows_tiles):
            pp = (base + rt) % 2
            wst_, wqi_, wqb_ = wst[pp], wqi[pp], wqb[pp]
            rmax_, rtmp_, rrec_ = rmax[pp], rtmp[pp], rrec[pp]
            P.dma(q, wst_[:, 0:in_dim], src_ap[rt * 128:(rt + 1) * 128, :],
                  writes=(wst_.name,))
            P.op("vector", lambda e, w=in_dim, wst_=wst_, rmax_=rmax_: e.tensor_reduce(
                out=rmax_[:], in_=wst_[:, 0:w], axis=AX.X, op=ALU.max,
                apply_absolute_value=True), reads=(wst_.name,), writes=(rmax_.name,))
            P.op("vector", lambda e, rmax_=rmax_, rtmp_=rtmp_: e.reciprocal(
                out=rtmp_[:], in_=rmax_[:]),
                reads=(rmax_.name,), writes=(rtmp_.name,))
            P.op("vector", lambda e, rtmp_=rtmp_, rrec_=rrec_: e.tensor_scalar(
                out=rrec_[:], in0=rtmp_[:], scalar1=QMAX, scalar2=None, op0=ALU.mult),
                reads=(rtmp_.name,), writes=(rrec_.name,))
            P.op("vector", lambda e, rt=rt, rmax_=rmax_: e.tensor_scalar(
                out=col(scale_buf, rt), in0=rmax_[:], scalar1=1.0 / QMAX, scalar2=None,
                op0=ALU.mult), reads=(rmax_.name,), writes=(scale_buf.name,))
            P.op("scalar", lambda e, w=in_dim, wst_=wst_, wqi_=wqi_, rrec_=rrec_: e.activation(
                out=wqi_[:, 0:w], in_=wst_[:, 0:w], func=AF.Copy, scale=rrec_[:]),
                reads=(wst_.name, rrec_.name), writes=(wqi_.name,))
            P.op("vector", lambda e, w=in_dim, wqi_=wqi_, wqb_=wqb_: e.tensor_copy(
                out=wqb_[:, 0:w], in_=wqi_[:, 0:w]),
                reads=(wqi_.name,), writes=(wqb_.name,))
            for kb in range(0, in_c, 8):
                nb = min(8, in_c - kb)
                pt_, wtr_ = ps_tr[pp], wtr[pp]

                def trans(e, kb=kb, nb=nb, wqb_=wqb_, pt_=pt_):
                    for j in range(nb):
                        inst = e.transpose(pt_[:, j * 128:(j + 1) * 128],
                                           wqb_[:, (kb + j) * 128:(kb + j + 1) * 128], eye[:])
                    return inst
                P.op("tensor", trans, reads=(wqb_.name, "eye"), writes=(pt_.name,))
                P.op("vector", lambda e, nb=nb, pt_=pt_, wtr_=wtr_: e.tensor_copy(
                    out=wtr_[:, 0:nb * 128], in_=pt_[:, 0:nb * 128]),
                    reads=(pt_.name,), writes=(wtr_.name,))
                P.dma("sync",
                      scratch.ap()[rt, :, kb * 128:(kb + nb) * 128],
                      wtr_[:, 0:nb * 128],
                      reads=(wtr_.name,), writes=(f"scr_{tag}{rt}",))
    def run_w1d():
        _sub = [f"{n}{i}{k}" for n in ("wst", "wqi", "wqb") for i in range(2)
                for k in ("g", "u")]
        P.barrier_wait("gpsimd", tuple(_sub))
        quant_weight(d_wd, HC, I_FFN, sdr, s_wdT, "wd", q="gpsimd")
        _w1 = [f"{n}{i}" for n in ("wst", "wqi", "wqb", "wtr", "rmax", "rtmp",
                                   "rrec", "umax", "utmp", "urec", "ps_tr")
               for i in range(2)]
        M.free(*_w1, P=P, keys=tuple(_w1) + tuple(_sub))


    sqtA = M.sb("sqtA", [128, TS], BF16)
    sqtB = M.sb("sqtB", [128, TS], BF16)
    rowt = [M.sb(f"rowt{j}", [1, TS], F32) for j in range(7)]
    rowsb0 = M.sb("rowsb0", [1, TS], BF16)
    rowsb1 = M.sb("rowsb1", [1, TS], BF16)
    zmx = M.sb("zmx", [128, 40], F32)
    amx = M.sb("amx", [128, TCN * IC], F32)
    zred = M.sb("zred", [128, 1], F32)
    gq = M.sb("gq", [1, 1], F32)
    redrow = M.sb("redrow", [1, 128], F32)
    sct = M.sb("sct", [1, 4], F32)
    scc = M.sb("scc", [128, 4], F32)
    i16t = [M.sb(f"i16t_{i}", [128, TS], I16) for i in range(4)]
    tz1 = [M.sb(f"tz1_{i}", [128, TS], BF16) for i in range(2)]
    rbsb = M.sb("rbsb", [128, TS], BF16)
    mbsb = M.sb("mbsb", [128, TS], BF16)
    ps_stat = M.ps("ps_stat", [33, TS], F32)
    ps_bc = M.ps("ps_bc", [128, TS], F32)

    def r_sl(j, tc):
        return rowt[j][0:1, 0:TS]

    def rb_sl(j, tc):
        t_ = rowsb0 if j == 0 else rowsb1
        return t_[0:1, 0:TS]

    def ln_stats_and_z(xt, xkey, zb, zkey, do_max, tag):
        for tc in range(TCN):
            ln_tc(tc, xt, xkey, zb, zkey, do_max)

    def ln_tc(tc, xt, xkey, zb, zkey, do_max):
        if True:
            xm = lambda hc, tc=tc: xt[:, hc * T + tc * TS: hc * T + (tc + 1) * TS]

            def mm_sx(e, tc=tc, xm=xm):
                for hc in range(HC):
                    i = e.matmul(ps_stat[0:1, :], onesc[:], xm(hc), start=(hc == 0), stop=(hc == HC - 1))
                return i
            P.op("tensor", mm_sx, reads=(xkey, "onesc"), writes=("ps_sx",))
            for hc in range(HC):
                sq = sqtA if hc % 2 == 0 else sqtB
                P.op("scalar", lambda e, hc=hc, sq=sq, xm=xm: e.activation(
                    out=sq[:], in_=xm(hc), func=AF.Square),
                    reads=(xkey,), writes=(sq.name,))
                P.op("tensor", lambda e, hc=hc, sq=sq: e.matmul(
                    ps_stat[32:33, :], onesc[:], sq[:], start=(hc == 0), stop=(hc == HC - 1)),
                    reads=(sq.name, "onesc"), writes=("ps_sxx",))
            P.op("vector", lambda e, tc=tc: e.tensor_scalar(
                out=r_sl(0, tc), in0=ps_stat[0:1, :], scalar1=1.0 / H, scalar2=None, op0=ALU.mult),
                reads=("ps_sx",), writes=("rows0",))
            P.op("vector", lambda e, tc=tc: e.tensor_scalar(
                out=r_sl(2, tc), in0=ps_stat[32:33, :], scalar1=1.0 / H, scalar2=None, op0=ALU.mult),
                reads=("ps_sxx",), writes=("rows2",))
            P.op("vector", lambda e, tc=tc: e.tensor_tensor(
                out=r_sl(3, tc), in0=r_sl(0, tc), in1=r_sl(0, tc), op=ALU.mult),
                reads=("rows0",), writes=("rows3",))
            P.op("vector", lambda e, tc=tc: e.scalar_tensor_tensor(
                out=r_sl(4, tc), in0=r_sl(2, tc), scalar=1e-5, in1=r_sl(3, tc),
                op0=ALU.add, op1=ALU.subtract),
                reads=("rows2", "rows3"), writes=("rows4",))
            P.op("scalar", lambda e, tc=tc: e.activation(
                out=r_sl(6, tc), in_=r_sl(4, tc), func=AF.Sqrt),
                reads=("rows4",), writes=("rows6",))
            P.op("vector", lambda e, tc=tc: e.reciprocal(
                out=r_sl(1, tc), in_=r_sl(6, tc)),
                reads=("rows6",), writes=("rows1",))
            P.op("vector", lambda e, tc=tc: e.tensor_tensor(
                out=r_sl(5, tc), in0=r_sl(0, tc), in1=r_sl(1, tc), op=ALU.mult),
                reads=("rows0", "rows1"), writes=("rows5",))
            P.op("vector", lambda e, tc=tc: e.tensor_copy(
                out=rb_sl(0, tc), in_=r_sl(1, tc)), reads=("rows1",), writes=("rowsb0",))
            P.op("vector", lambda e, tc=tc: e.tensor_copy(
                out=rb_sl(1, tc), in_=r_sl(5, tc)), reads=("rows5",), writes=("rowsb1",))
            P.op("tensor", lambda e, tc=tc: e.matmul(
                ps_bc[:], ones1[:], rb_sl(0, tc), start=True, stop=True),
                reads=("rowsb0", "ones1"), writes=("ps_bc",))
            P.op("vector", lambda e: e.tensor_copy(out=rbsb[:], in_=ps_bc[:]),
                 reads=("ps_bc",), writes=("rbsb",))
            P.op("tensor", lambda e, tc=tc: e.matmul(
                ps_bc[:], ones1[:], rb_sl(1, tc), start=True, stop=True),
                reads=("rowsb1", "ones1"), writes=("ps_bc",))
            P.op("vector", lambda e: e.tensor_copy(out=mbsb[:], in_=ps_bc[:]),
                 reads=("ps_bc",), writes=("mbsb",))
            for hc in range(HC):
                t_ = tz1[hc % 2]
                P.op("vector", lambda e, hc=hc, xm=xm, t_=t_: e.tensor_tensor(
                    out=t_[:], in0=xm(hc), in1=rbsb[:], op=ALU.mult),
                    reads=(xkey, "rbsb"), writes=(t_.name,))
                P.op("vector", lambda e, hc=hc, tc=tc, t_=t_: e.tensor_tensor(
                    out=zb[:, hc * T + tc * TS: hc * T + (tc + 1) * TS],
                    in0=t_[:], in1=mbsb[:], op=ALU.subtract),
                    reads=(t_.name, "mbsb"), writes=(zkey,))
                if do_max:
                    P.op("vector", lambda e, hc=hc, tc=tc: e.tensor_reduce(
                        out=col(zmx, tc * HC + hc),
                        in_=zb[:, hc * T + tc * TS: hc * T + (tc + 1) * TS],
                        axis=AX.X, op=ALU.max, apply_absolute_value=True),
                        reads=(zkey,), writes=("zmx",))


    def part_max_to_sct3(src_key):
        P.dma("sync", redrow[0:1, 0:128], zred[:, 0:1], reads=(src_key,), writes=("redrow",))
        P.op("vector", lambda e: e.tensor_reduce(
            out=sct[0:1, 3:4], in_=redrow[:], axis=AX.X, op=ALU.max),
            reads=("redrow",), writes=("sct3",))

    def col_bcast(dst_col, dst_key, src_sl, src_key, pcol):
        P.op("tensor", lambda e: e.matmul(
            ps_bc[:, pcol:pcol + 1], ones1f[:], src_sl, start=True, stop=True),
            reads=(src_key, "ones1f"), writes=(f"ps_bc{pcol}",))
        P.op("vector", lambda e: e.tensor_copy(out=dst_col, in_=ps_bc[:, pcol:pcol + 1]),
             reads=(f"ps_bc{pcol}",), writes=(dst_key,))

    def global_max_scale(idx, n_cols, src_key, sig, tag, mid=None):
        P.op("vector", lambda e: e.tensor_reduce(
            out=zred[:], in_=zmx[:, 0:n_cols], axis=AX.X, op=ALU.max),
            reads=(src_key,), writes=("zred",))
        part_max_to_sct3("zred")
        P.dma("sync", cin_mx[idx], sct[0:1, 3:4], reads=("sct3",), writes=(f"cinmx{idx}",))
        P.op("gpsimd", lambda e: e.collective_compute(
            "AllReduce", ALU.max, ins=[cin_mx[idx]], outs=[cout_mx[idx]],
            replica_groups=RG), reads=(f"cinmx{idx}",), writes=(f"coutmx{idx}",))
        if mid is not None:
            mid()
        P.dma("sync", gq[:], cout_mx[idx], reads=(f"coutmx{idx}",), writes=("gq",))
        P.op("vector", lambda e, sig=sig: e.tensor_tensor(
            out=sct[0:1, 0:1], in0=gq[:], in1=sig[:], op=ALU.mult),
            reads=("gq", sig.name), writes=("sct0",))
        P.op("vector", lambda e: e.tensor_scalar(
            out=sct[0:1, 1:2], in0=sct[0:1, 0:1], scalar1=1.0 / QMAX, scalar2=None,
            op0=ALU.mult), reads=("sct0",), writes=("sct1",))
        P.op("vector", lambda e: e.reciprocal(
            out=sct[0:1, 2:3], in_=sct[0:1, 1:2]),
            reads=("sct1",), writes=("sct2",))
        col_bcast(col(scc, 0), "scc0", sct[0:1, 1:2], "sct1", 0)
        col_bcast(col(scc, 1), "scc1", sct[0:1, 2:3], "sct2", 1)

    def ffn(f, xt, xkey, x_new, x_new_key, final, zb_pre=None, after_tc=None,
            post_gu=None):
        zkey = f"zb{f}"
        if zb_pre is None:
            zb = M.sb(f"zb{f}", [128, HC * T], BF16, side="right")
            ln_stats_and_z(xt, xkey, zb, zkey, True, f"f{f}")
        else:
            zb = zb_pre
        P.dma("sync", s_xr.ap(), xt[:], reads=(xkey,), writes=(f"s_xr{f}",))
        M.free(xt.name, P=P, keys=(xkey,))
        if "z1" in taps and f == 0:
            P.dma("sync", d_taps["z1"], zb[:], reads=(zkey,), writes=("tapz1",))
        mxidx = 2 * f
        wgt = [M.sb(f"wgt{f}_{i}", [128, HC * 128], BF16) for i in range(3)]
        wut = [M.sb(f"wut{f}_{i}", [128, HC * 128], BF16) for i in range(3)]
        if f == 0:
            def _wq_env(ic, kind):
                st_, qi_, qb_ = wst[ic % 2], wqi[ic % 2], wqb[ic % 2]
                if kind == "g":
                    off = 0
                    rm_, rt_, rr_ = rmax[ic % 2], rtmp[ic % 2], rrec[ic % 2]
                    src_, ring, scr_, scol = d_wg, wgt, s_wgT, sgr
                    qin, tag, pt_ = "sync", "wg", ps_tr[0]
                else:
                    off = H
                    rm_, rt_, rr_ = umax[ic % 2], utmp[ic % 2], urec[ic % 2]
                    src_, ring, scr_, scol = d_wu, wut, s_wuT, sur
                    qin, tag, pt_ = "scalar", "wu", ps_tr[1]
                kst, kqi, kqb = (st_.name + kind), (qi_.name + kind), (qb_.name + kind)
                return (st_, qi_, qb_, off, rm_, rt_, rr_, src_, ring, scr_, scol,
                        qin, tag, pt_, kst, kqi, kqb)

            def fused_w1(ic, kind):
                (st_, qi_, qb_, off, rm_, rt_, rr_, src_, ring, scr_, scol,
                 qin, tag, pt_, kst, kqi, kqb) = _wq_env(ic, kind)
                osl = slice(off, off + H)
                P.dma(qin, st_[:, osl], src_[ic * 128:(ic + 1) * 128, :],
                      reads=(kst,), writes=(kst,))
                P.op("vector", lambda e, st_=st_, rm_=rm_, osl=osl: e.tensor_reduce(
                    out=rm_[:], in_=st_[:, osl], axis=AX.X, op=ALU.max,
                    apply_absolute_value=True), reads=(kst,), writes=(rm_.name,))
                P.op("vector", lambda e, rm_=rm_, rt_=rt_: e.reciprocal(
                    out=rt_[:], in_=rm_[:]), reads=(rm_.name,), writes=(rt_.name,))
                P.op("vector", lambda e, rt_=rt_, rr_=rr_: e.tensor_scalar(
                    out=rr_[:], in0=rt_[:], scalar1=QMAX, scalar2=None,
                    op0=ALU.mult), reads=(rt_.name,), writes=(rr_.name,))
                P.op("vector", lambda e, ic=ic, rm_=rm_, scol=scol: e.tensor_scalar(
                    out=col(scol, ic), in0=rm_[:], scalar1=1.0 / QMAX, scalar2=None,
                    op0=ALU.mult), reads=(rm_.name,), writes=(scol.name,))
                P.op("scalar", lambda e, st_=st_, qi_=qi_, rr_=rr_, osl=osl: e.activation(
                    out=qi_[:, osl], in_=st_[:, osl], func=AF.Copy, scale=rr_[:]),
                    reads=(kst, rr_.name), writes=(kqi,))
                P.op("vector", lambda e, qi_=qi_, qb_=qb_, osl=osl: e.tensor_scalar(
                    out=qb_[:, osl], in0=qi_[:, osl], scalar1=-128, scalar2=127,
                    op0=ALU.max, op1=ALU.min), reads=(kqi,), writes=(kqb,))

            def fused_w2(ic, kind):
                (st_, qi_, qb_, off, rm_, rt_, rr_, src_, ring, scr_, scol,
                 qin, tag, pt_, kst, kqi, kqb) = _wq_env(ic, kind)
                dst = ring[ic % 3]

                def trans(e, qb_=qb_, pt_=pt_, off=off):
                    for j in range(HC):
                        inst = e.transpose(pt_[:, j * 128:(j + 1) * 128],
                                           qb_[:, off + j * 128:off + (j + 1) * 128],
                                           eye[:])
                    return inst
                P.op("tensor", trans, reads=(kqb, "eye"), writes=(pt_.name,))
                P.op("vector", lambda e, pt_=pt_, dst=dst: e.tensor_copy(
                    out=dst[:], in_=pt_[:]), reads=(pt_.name,), writes=(dst.name,))
                P.dma("gpsimd", scr_.ap()[ic], dst[:],
                      reads=(dst.name,), writes=(f"scr_{tag}{ic}",))

            def fused_wtile(ic, kind):
                fused_w1(ic, kind)
                fused_w2(ic, kind)

            def head_start():
                for ic in (0, 1):
                    fused_wtile(ic, "g")
                    fused_wtile(ic, "u")
            global_max_scale(mxidx, 4 * HC, "zmx", sig_x, f"fx{f}", mid=head_start)
        else:
            global_max_scale(mxidx, 4 * HC, "zmx", sig_x, f"fx{f}")
        qof = 8 * f
        P.op("vector", lambda e: e.tensor_copy(out=col(qcols, qof + 0), in_=col(scc, 1)),
             reads=("scc1",), writes=(f"qc{f}0",))
        if f == 1:
            P.op("vector", lambda e: e.tensor_scalar(
                out=sgx[:, f * IC:(f + 1) * IC], in0=sgr[:], scalar1=col(scc, 0),
                scalar2=None, op0=ALU.mult), reads=("scc0", "sgr"), writes=(f"sgx{f}",))
            P.op("vector", lambda e: e.tensor_scalar(
                out=sux[:, f * IC:(f + 1) * IC], in0=sur[:], scalar1=col(scc, 0),
                scalar2=None, op0=ALU.mult), reads=("scc0", "sur"), writes=(f"sux{f}",))
        xqb = M.sb(f"xqb{f}", [128, HC * T], BF16)
        xqkey = f"xqb{f}"
        for tc in range(TCN):
            for hc in range(HC):
                pp = (tc * HC + hc) % 4
                i16_ = i16t[pp]
                sl = slice(hc * T + tc * TS, hc * T + (tc + 1) * TS)
                P.op("scalar", lambda e, sl=sl, i16_=i16_: e.activation(
                    out=i16_[:], in_=zb[:, sl], func=AF.Copy, scale=col(qcols, qof + 0)),
                    reads=(zkey, f"qc{f}0"), writes=(i16_.name,))
                P.op("vector", lambda e, sl=sl, i16_=i16_, tc=tc: e.tensor_scalar(
                    out=xqb[:, sl], in0=i16_[:], scalar1=-128, scalar2=127,
                    op0=ALU.max, op1=ALU.min),
                    reads=(i16_.name,), writes=(f"xq{f}t{tc}",))
        M.free(f"zb{f}", P=P, keys=(zkey,))
        if "xq1" in taps and f == 0:
            P.dma("sync", d_taps["xq1"], xqb[:],
                  reads=tuple(f"xq{f}t{tc}" for tc in range(TCN)), writes=("tapxq",))
        abw = [M.sb(f"abw{f}_{i}", [128, TS], BF16) for i in range(2)]
        psg = [M.ps(f"psg{f}_{i}", [128, TS], F32) for i in range(2)]
        psu = [M.ps(f"psu{f}_{i}", [128, TS], F32) for i in range(2)]
        sil = [M.sb(f"sil{f}_{i}", [128, TS], BF16) for i in range(2)]
        upt = [M.sb(f"upt{f}_{i}", [128, TS], BF16) for i in range(2)]
        for ic in range(IC):
            wg_b, wu_b = wgt[ic % 3], wut[ic % 3]
            if f == 0:
                if ic + 2 < IC:
                    fused_w1(ic + 2, "g")
                    fused_w1(ic + 2, "u")
                P.op("vector", lambda e, ic=ic: e.tensor_scalar(
                    out=col(sgx, ic), in0=col(sgr, ic), scalar1=col(scc, 0),
                    scalar2=None, op0=ALU.mult),
                    reads=("sgr", "scc0"), writes=(f"sgx{f}",))
                P.op("vector", lambda e, ic=ic: e.tensor_scalar(
                    out=col(sux, ic), in0=col(sur, ic), scalar1=col(scc, 0),
                    scalar2=None, op0=ALU.mult),
                    reads=("sur", "scc0"), writes=(f"sux{f}",))
            else:
                P.dma("gpsimd", wg_b[:], s_wgT.ap()[ic],
                      reads=(f"scr_wg{ic}",), writes=(wg_b.name,))
                P.dma("gpsimd", wu_b[:], s_wuT.ap()[ic],
                      reads=(f"scr_wu{ic}",), writes=(wu_b.name,))
            for tc in range(TCN):
                pp = (ic * TCN + tc) % 2
                pg_, pu_, sl_ = psg[pp], psu[pp], sil[pp]

                def mmg(e, wb=wg_b, tc=tc, pg_=pg_):
                    for hc in range(HC):
                        i = e.matmul(pg_[:], wb[:, hc * 128:(hc + 1) * 128],
                                     xqb[:, hc * T + tc * TS: hc * T + (tc + 1) * TS],
                                     start=(hc == 0), stop=(hc == HC - 1))
                    return i
                P.op("tensor", mmg, reads=(wg_b.name, f"xq{f}t{tc}"), writes=(pg_.name,))

                def mmu(e, wb=wu_b, tc=tc, pu_=pu_):
                    for hc in range(HC):
                        i = e.matmul(pu_[:], wb[:, hc * 128:(hc + 1) * 128],
                                     xqb[:, hc * T + tc * TS: hc * T + (tc + 1) * TS],
                                     start=(hc == 0), stop=(hc == HC - 1))
                    return i
                P.op("tensor", mmu, reads=(wu_b.name, f"xq{f}t{tc}"), writes=(pu_.name,))
                P.op("scalar", lambda e, ic=ic, pg_=pg_, sl_=sl_: e.activation(
                    out=sl_[:], in_=pg_[:], func=AF.Silu, scale=col(sgx, f * IC + ic)),
                    reads=(pg_.name, f"sgx{f}"), writes=(sl_.name,))
                up_ = upt[pp]
                P.op("scalar", lambda e, ic=ic, pu_=pu_, up_=up_: e.activation(
                    out=up_[:], in_=pu_[:], func=AF.Identity, scale=col(sux, f * IC + ic)),
                    reads=(pu_.name, f"sux{f}"), writes=(up_.name,))
                asl = slice(ic * T + tc * TS, ic * T + (tc + 1) * TS)
                aw_ = abw[pp]
                P.op("vector", lambda e, aw_=aw_, up_=up_, sl_=sl_: e.tensor_tensor(
                    out=aw_[:], in0=up_[:], in1=sl_[:], op=ALU.mult),
                    reads=(up_.name, sl_.name), writes=(aw_.name,))
                P.op("vector", lambda e, ic=ic, tc=tc, aw_=aw_: e.tensor_reduce(
                    out=col(amx, tc * IC + ic), in_=aw_[:],
                    axis=AX.X, op=ALU.max, apply_absolute_value=True),
                    reads=(aw_.name,), writes=("amx",))
                P.dma("scalar", s_ab.ap()[:, asl], aw_[:],
                      reads=(aw_.name,), writes=(f"sab{ic}_{tc}",))
            if f == 0 and ic + 2 < IC:
                fused_w2(ic + 2, "g")
                fused_w2(ic + 2, "u")
        _g = [f"xqb{f}", f"wgt{f}_0", f"wgt{f}_1", f"wgt{f}_2",
              f"wut{f}_0", f"wut{f}_1", f"wut{f}_2",
              f"psg{f}_0", f"psg{f}_1", f"psu{f}_0", f"psu{f}_1",
              f"sil{f}_0", f"sil{f}_1", f"upt{f}_0", f"upt{f}_1",
              f"abw{f}_0", f"abw{f}_1"]
        _gk = _g[1:] + [f"xq{f}t{tc}" for tc in range(TCN)]
        M.free(*_g, P=P, keys=tuple(_gk))
        if f == 0:
            run_w1d()
        if post_gu is not None:
            post_gu()
        P.op("vector", lambda e: e.tensor_reduce(
            out=zred[:], in_=amx[:, 0:TCN * IC], axis=AX.X, op=ALU.max),
            reads=("amx",), writes=("zred",))
        part_max_to_sct3("zred")
        idx = 2 * f + 1
        P.dma("sync", cin_mx[idx], sct[0:1, 3:4], reads=("sct3",), writes=(f"cinmx{idx}",))
        P.op("gpsimd", lambda e, idx=idx: e.collective_compute(
            "AllReduce", ALU.max, ins=[cin_mx[idx]], outs=[cout_mx[idx]],
            replica_groups=RG), reads=(f"cinmx{idx}",), writes=(f"coutmx{idx}",))
        P.dma("sync", gq[:], cout_mx[idx], reads=(f"coutmx{idx}",), writes=("gq",))
        P.op("vector", lambda e: e.tensor_tensor(
            out=sct[0:1, 0:1], in0=gq[:], in1=sig_h[:], op=ALU.mult),
            reads=("gq", "sig_h"), writes=("sct0",))
        P.op("vector", lambda e: e.tensor_scalar(
            out=sct[0:1, 1:2], in0=sct[0:1, 0:1], scalar1=1.0 / QMAX, scalar2=None,
            op0=ALU.mult), reads=("sct0",), writes=("sct1",))
        P.op("vector", lambda e: e.reciprocal(
            out=sct[0:1, 2:3], in_=sct[0:1, 1:2]),
            reads=("sct1",), writes=("sct2",))
        col_bcast(col(scc, 0), "scc0", sct[0:1, 1:2], "sct1", 0)
        col_bcast(col(scc, 1), "scc1", sct[0:1, 2:3], "sct2", 1)
        P.op("vector", lambda e: e.tensor_copy(out=col(qcols, qof + 3), in_=col(scc, 1)),
             reads=("scc1",), writes=(f"qc{f}3",))
        P.op("vector", lambda e: e.tensor_scalar(
            out=sdh[:, f * HC:(f + 1) * HC], in0=sdr[:], scalar1=col(scc, 0),
            scalar2=0.5, op0=ALU.mult, op1=ALU.mult),
            reads=("scc0", "sdr"), writes=(f"sdh{f}",))
        obuf = [M.sb(f"obuf{f}_{i}", [128, TS], F32) for i in range(2)] if final else None
        abr = [M.sb(f"abr{f}_{i}", [128, TS], BF16) for i in range(4)]
        hqb = M.sb(f"hqb{f}", [128, IC * TS], BF16)
        wdt = [M.sb(f"wdt{f}_{i}", [128, IC * 128], BF16) for i in range(2)]
        psd = [M.ps(f"psd{f}_{i}", [128, TS], F32) for i in range(2)]
        xrr = [M.sb(f"xrr{f}_{i}", [128, TS], BF16) for i in range(2)]
        for tc in range(TCN):
            for ic in range(IC):
                pp = (tc * IC + ic) % 4
                i16_ = i16t[pp]
                ar_ = abr[pp]
                asl = slice(ic * T + tc * TS, ic * T + (tc + 1) * TS)
                P.dma(("sync", "gpsimd")[ic % 2], ar_[:], s_ab.ap()[:, asl],
                      reads=(f"sab{ic}_{tc}",), writes=(ar_.name,))
                P.op("scalar", lambda e, ar_=ar_, i16_=i16_: e.activation(
                    out=i16_[:], in_=ar_[:], func=AF.Copy, scale=col(qcols, qof + 3)),
                    reads=(ar_.name, f"qc{f}3"), writes=(i16_.name,))
                P.op("vector", lambda e, ic=ic, i16_=i16_: e.tensor_scalar(
                    out=hqb[:, ic * TS:(ic + 1) * TS], in0=i16_[:],
                    scalar1=-128, scalar2=127, op0=ALU.max, op1=ALU.min),
                    reads=(i16_.name,), writes=(f"hqb{f}c{ic // 6}",))
            for hc in range(HC):
                wd_b = wdt[hc % 2]
                P.dma("gpsimd", wd_b[:], s_wdT.ap()[hc],
                      reads=(f"scr_wd{hc}",), writes=(wd_b.name,))
                xr_ = xrr[hc % 2]
                P.dma("sync", xr_[:],
                      s_xr.ap()[:, hc * T + tc * TS: hc * T + (tc + 1) * TS],
                      reads=(f"s_xr{f}",), writes=(xr_.name,))

                pd_ = psd[hc % 2]
                for g4 in range(4):
                    ic_lo, ic_hi = g4 * 6, min((g4 + 1) * 6, IC)

                    def mmd(e, wb=wd_b, pd_=pd_, ic_lo=ic_lo, ic_hi=ic_hi):
                        for ic in range(ic_lo, ic_hi):
                            i = e.matmul(pd_[:], wb[:, ic * 128:(ic + 1) * 128],
                                         hqb[:, ic * TS:(ic + 1) * TS],
                                         start=(ic == 0), stop=(ic == IC - 1))
                        return i
                    P.op("tensor", mmd, reads=(wd_b.name, f"hqb{f}c{g4}"),
                         writes=(pd_.name,))
                xsl = slice(hc * T + tc * TS, hc * T + (tc + 1) * TS)
                if final:
                    ob_ = obuf[hc % 2]
                    P.op("vector", lambda e, hc=hc, xr_=xr_, pd_=pd_, ob_=ob_: e.scalar_tensor_tensor(
                        out=ob_[:], in0=pd_[:], scalar=col(sdh, f * HC + hc),
                        in1=xr_[:], op0=ALU.mult, op1=ALU.add),
                        reads=(pd_.name, f"sdh{f}", xr_.name), writes=(ob_.name,))
                    P.dma("sync",
                          d_y[hc * 128:(hc + 1) * 128, tc * TS:(tc + 1) * TS],
                          ob_[:], reads=(ob_.name,), writes=("y_out",))
                else:
                    P.op("vector", lambda e, hc=hc, xsl=xsl, xr_=xr_, pd_=pd_: e.scalar_tensor_tensor(
                        out=x_new[:, xsl], in0=pd_[:], scalar=col(sdh, f * HC + hc),
                        in1=xr_[:], op0=ALU.mult, op1=ALU.add),
                        reads=(pd_.name, f"sdh{f}", xr_.name), writes=(x_new_key,))
            if after_tc is not None:
                after_tc(tc)
        fl = [f"abr{f}_0", f"abr{f}_1", f"abr{f}_2", f"abr{f}_3",
              f"hqb{f}", f"wdt{f}_0", f"wdt{f}_1",
              f"psd{f}_0", f"psd{f}_1",
              f"xrr{f}_0", f"xrr{f}_1"] + ([f"obuf{f}_0", f"obuf{f}_1"] if final else [])
        M.free(*fl, P=P, keys=tuple(fl))

    wst = [M.sb(f"wst{i}", [128, I_FFN], F32) for i in range(2)]
    wqi = [M.sb(f"wqi{i}", [128, I_FFN], I16) for i in range(2)]
    wqb = [M.sb(f"wqb{i}", [128, I_FFN], BF16) for i in range(2)]
    wtr = [M.sb(f"wtr{i}", [128, 1024], BF16) for i in range(2)]
    rmax = [M.sb(f"rmax{i}", [128, 1], F32) for i in range(2)]
    rtmp = [M.sb(f"rtmp{i}", [128, 1], F32) for i in range(2)]
    rrec = [M.sb(f"rrec{i}", [128, 1], F32) for i in range(2)]
    umax = [M.sb(f"umax{i}", [128, 1], F32) for i in range(2)]
    utmp = [M.sb(f"utmp{i}", [128, 1], F32) for i in range(2)]
    urec = [M.sb(f"urec{i}", [128, 1], F32) for i in range(2)]
    ps_tr = [M.ps(f"ps_tr{i}", [128, 1024], BF16) for i in range(2)]

    x0 = M.sb("x0", [128, HC * T], BF16)
    xf32 = [M.sb(f"xf32_{i}", [128, TS], F32) for i in range(4)]
    for tc in range(TCN):
        for hc in range(HC):
            xf = xf32[(tc * HC + hc) % 4]
            P.dma(("sync", "scalar")[hc % 2], xf[:],
                  d_x[hc * 128:(hc + 1) * 128, tc * TS:(tc + 1) * TS],
                  reads=(xf.name,), writes=(xf.name,))
            P.op("vector", lambda e, hc=hc, tc=tc, xf=xf: e.tensor_copy(
                out=x0[:, hc * T + tc * TS: hc * T + (tc + 1) * TS], in_=xf[:]),
                reads=(xf.name,), writes=("x0",))
    M.free("xf32_0", "xf32_1", "xf32_2", "xf32_3", P=P,
           keys=("xf32_0", "xf32_1", "xf32_2", "xf32_3"))

    x1 = M.sb("x1", [128, HC * T], BF16, side="right")
    zb2_box = {}

    def mk_zb2():
        zb2_box["t"] = M.sb("zb2", [128, HC * T], BF16)
    ffn(0, x0, "x0", x1, "x1", final=False, post_gu=mk_zb2,
        after_tc=lambda tc: ln_tc(tc, x1, "x1", zb2_box["t"], "zb2", False))
    zb2 = zb2_box["t"]
    if "x1" in taps:
        P.dma("sync", d_taps["x1"], x1[:], reads=("x1",), writes=("tapx1",))
    P.dma("sync", s_x1.ap(), x1[:], reads=("x1",), writes=("s_x1",))
    M.free("x1", P=P, keys=("x1",))

    xin = M.sb("xin", [128, DC * T], BF16, side="right")
    wif = [M.sb(f"wif{i}", [128, HC * 128], F32) for i in range(2)]
    wib = [M.sb(f"wib{i}", [128, HC * 128], BF16) for i in range(2)]
    psi = [M.ps(f"psi{i}", [128, TS], F32) for i in range(2)]
    srt = [M.sb(f"srt{i}", [128, T], BF16) for i in range(2)]
    for oc in range(2 * DC):
        wb, wf = wib[oc % 2], wif[oc % 2]
        P.dma("gpsimd", wf[:], d_win[oc], reads=(wf.name,), writes=(wf.name,))
        P.op("vector", lambda e, wb=wb, wf=wf: e.tensor_copy(out=wb[:], in_=wf[:]),
             reads=(wf.name,), writes=(wb.name,))
        for tc in range(TCN):
            pi_ = psi[tc % 2]

            def mmi(e, wb=wb, tc=tc, pi_=pi_):
                for hc in range(HC):
                    i = e.matmul(pi_[:], wb[:, hc * 128:(hc + 1) * 128],
                                 zb2[:, hc * T + tc * TS: hc * T + (tc + 1) * TS],
                                 start=(hc == 0), stop=(hc == HC - 1))
                return i
            P.op("tensor", mmi, reads=(wb.name, "zb2"), writes=(pi_.name,))
            if oc < DC:
                P.op("scalar", lambda e, oc=oc, tc=tc, pi_=pi_: e.activation(
                    out=xin[:, oc * T + tc * TS: oc * T + (tc + 1) * TS],
                    in_=pi_[:], func=AF.Identity, bias=col(c_bin, oc)),
                    reads=(pi_.name, "c_bin"), writes=("xin",))
            else:
                sb_ = srt[oc % 2]
                P.op("scalar", lambda e, oc=oc, tc=tc, sb_=sb_, pi_=pi_: e.activation(
                    out=sb_[:, tc * TS:(tc + 1) * TS], in_=pi_[:], func=AF.Silu,
                    bias=col(c_bin, oc)),
                    reads=(pi_.name, "c_bin"), writes=(sb_.name,))
        if oc >= DC:
            sb_ = srt[oc % 2]
            P.dma("sync", s_sres.ap()[:, (oc - DC) * T:(oc - DC + 1) * T], sb_[:],
                  reads=(sb_.name,), writes=("s_sres",))
        if oc == DC - 1:
            bndst = M.sb("bndst", [128, DC * 3], BF16, side="right")
            P.op("vector", lambda e: e.tensor_copy(
                out=bndst.ap().rearrange("p (d c) -> p d c", d=DC),
                in_=xin.ap().rearrange("p (d t) -> p d t", d=DC)[:, :, T - 3:T]),
                reads=("xin",), writes=("bndst",))
            P.dma("sync", cin_bnd, bndst[:], reads=("bndst",), writes=("cin_bnd",))
            P.op("gpsimd", lambda e: e.collective_compute(
                "AllGather", ALU.bypass, ins=[cin_bnd], outs=[cout_bnd],
                replica_groups=RG), reads=("cin_bnd",), writes=("cout_bnd",))
    M.free("wif0", "wif1", "wib0", "wib1", "srt0", "srt1", "psi0", "psi1", P=P,
           keys=("wif0", "wif1", "wib0", "wib1", "srt0", "srt1", "psi0", "psi1"))
    M.free("zb2", P=P, keys=("zb2",))
    if "xin" in taps:
        P.dma("sync", d_taps["xin"], xin[:], reads=("xin",), writes=("tapxin",))

    bndr = M.sb("bndr", [128, DC * 3], BF16, side="right")
    bndm = M.sb("bndm", [128, DC * 3], BF16, side="right")

    def dyn_bnd(e):
        return e.dma_start(
            out=bndr[:],
            in_=bass.AP(cout_bnd.tensor, regs["bnd"], [[DC * 3, 128], [1, DC * 3]]))
    P.dyn_dma("gpsimd", dyn_bnd, reads=("cout_bnd",), writes=("bndr",))
    P.op("vector", lambda e: e.tensor_scalar(
        out=bndm[:], in0=bndr[:], scalar1=col(cmask, 0), scalar2=None, op0=ALU.mult),
        reads=("bndr",), writes=("bndm",))

    xe = M.sb("xe", [128, T + 3], BF16)
    cv1 = M.sb("cv1", [128, T], BF16)
    cv2 = M.sb("cv2", [128, T], BF16)
    cv3 = M.sb("cv3", [128, T], BF16)
    for dc in range(DC):
        P.op("vector", lambda e, dc=dc: e.tensor_copy(
            out=xe[:, 0:3], in_=bndm[:, dc * 3:(dc + 1) * 3]),
            reads=("bndm",), writes=("xe",))
        P.op("vector", lambda e, dc=dc: e.tensor_copy(
            out=xe[:, 3:T + 3], in_=xin[:, dc * T:(dc + 1) * T]),
            reads=("xin",), writes=("xe",))
        P.op("vector", lambda e, dc=dc: e.tensor_scalar(
            out=cv1[:], in0=xe[:, 0:T], scalar1=col(c_cw, dc * 4 + 0), scalar2=None,
            op0=ALU.mult), reads=("xe", "c_cw"), writes=("cv1",))
        for k in (1, 2, 3):
            s_, d_ = (cv1, cv2) if k % 2 == 1 else (cv2, cv1)
            P.op("vector", lambda e, dc=dc, k=k: e.tensor_scalar(
                out=cv3[:], in0=xe[:, k:T + k], scalar1=col(c_cw, dc * 4 + k),
                scalar2=None, op0=ALU.mult), reads=("xe", "c_cw"), writes=("cv3",))
            P.op("vector", lambda e, s_=s_, d_=d_: e.tensor_tensor(
                out=d_[:], in0=cv3[:], in1=s_[:], op=ALU.add),
                reads=("cv3", s_.name), writes=(d_.name,))
        P.op("scalar", lambda e, dc=dc: e.activation(
            out=xin[:, dc * T:(dc + 1) * T], in_=cv2[:], func=AF.Silu,
            bias=col(c_cb, dc)), reads=("cv2", "c_cb"), writes=("xin",))
    M.free("xe", "cv1", "cv2", "cv3", "bndr", "bndm", "bndst", P=P,
           keys=("xe", "cv1", "cv2", "cv3", "bndr", "bndm", "bndst"))
    if "xc" in taps:
        P.dma("sync", d_taps["xc"], xin[:], reads=("xin",), writes=("tapxc",))

    xpw_b = M.sb("xpw_b", [128, DC * 48], BF16)
    xpf = M.sb("xpf", [128, 48], F32, side="right")
    for dc in range(DC):
        P.dma("sync", xpf[:], d_xpw[dc * 128:(dc + 1) * 128, :], writes=("xpf",))
        P.op("vector", lambda e, dc=dc: e.tensor_copy(
            out=xpw_b[:, dc * 48:(dc + 1) * 48], in_=xpf[:]),
            reads=("xpf",), writes=("xpw_b",))
    dtw_f = M.sb("dtw_f", [NST, D_IN], F32, side="right")
    dtw_b = M.sb("dtw_b", [NST, D_IN], BF16)
    P.dma("sync", dtw_f[:], d_dtw, writes=("dtw_f",))
    P.op("vector", lambda e: e.tensor_copy(out=dtw_b[:], in_=dtw_f[:]),
         reads=("dtw_f",), writes=("dtw_b",))
    M.free("xpf", "dtw_f", P=P, keys=("xpf", "dtw_f"))

    prowf = M.sb("prowf", [NST, T], F32, side="right")
    draw_b = M.sb("draw_b", [NST, T], BF16)
    brow_b = M.sb("brow_b", [NST, T], BF16)
    crow_b = M.sb("crow_b", [NST, T], BF16)
    psp = M.ps("psp", [NST, TS], F32)
    for g, dst in ((0, draw_b), (1, brow_b), (2, crow_b)):
        for tc in range(TCN):
            def mmp(e, g=g, tc=tc):
                for dc in range(DC):
                    i = e.matmul(psp[:], xpw_b[:, dc * 48 + g * NST: dc * 48 + (g + 1) * NST],
                                 xin[:, dc * T + tc * TS: dc * T + (tc + 1) * TS],
                                 start=(dc == 0), stop=(dc == DC - 1))
                return i
            P.op("tensor", mmp, reads=("xpw_b", "xin"), writes=("psp",))
            P.op("scalar", lambda e, g=g, tc=tc: e.activation(
                out=prowf[:, tc * TS:(tc + 1) * TS], in_=psp[:], func=AF.Identity,
                bias=c_xpb3[0:NST, g:g + 1]), reads=("psp", "c_xpb3"), writes=("prowf",))
        P.op("vector", lambda e, dst=dst: e.tensor_copy(out=dst[:], in_=prowf[:]),
             reads=("prowf",), writes=(dst.name,))
    M.free("prowf", P=P, keys=("prowf",))
    P.dma("sync", s_bc.ap()[0], brow_b[:], reads=("brow_b",), writes=("s_bc",))
    P.dma("sync", s_bc.ap()[1], crow_b[:], reads=("crow_b",), writes=("s_bc",))
    M.free("brow_b", "crow_b", P=P, keys=("brow_b", "crow_b"))

    pbct = [M.sb(f"pbct{i}", [128, TS], BF16) for i in range(4)]
    rb1 = [M.sb(f"rb1_{i}", [1, T], BF16) for i in range(4)]
    ps_bb = [M.ps(f"ps_bb{i}", [128, TS], F32) for i in range(4)]
    rows = [(bc, n) for bc in range(2) for n in range(NST)]

    def row_dma(i):
        bc, n = rows[i]
        r_ = rb1[i % 4]
        P.dma("sync", r_[:], s_bc.ap()[bc, n], reads=(r_.name, "s_bc"),
              writes=(r_.name,))
    for i in range(3):
        row_dma(i)
    for i, (bc, n) in enumerate(rows):
        scr = s_Bb if bc == 0 else s_Cb
        skey = "s_Bb" if bc == 0 else "s_Cb"
        r_ = rb1[i % 4]
        if i + 3 < len(rows):
            row_dma(i + 3)
        for q in range(TCN):
            pb_, pq_ = pbct[q], ps_bb[q]
            P.op("tensor", lambda e, r_=r_, q=q, pq_=pq_: e.matmul(
                pq_[:], ones1[:], r_[0:1, q * TS:(q + 1) * TS],
                start=True, stop=True),
                reads=(r_.name, "ones1"), writes=(pq_.name,))
            if q % 2 == 0:
                P.op("scalar", lambda e, pb_=pb_, pq_=pq_: e.activation(
                    out=pb_[:], in_=pq_[:], func=AF.Identity),
                    reads=(pq_.name,), writes=(pb_.name,))
            else:
                P.op("vector", lambda e, pb_=pb_, pq_=pq_: e.tensor_copy(
                    out=pb_[:], in_=pq_[:]), reads=(pq_.name,), writes=(pb_.name,))
            P.dma(("scalar", "gpsimd")[q % 2],
                  scr.ap()[n, q // 2][:, (q % 2) * TS:(q % 2 + 1) * TS],
                  pb_[:], reads=(pb_.name,), writes=(skey,))
    M.free("pbct0", "pbct1", "pbct2", "pbct3", "rb1_0", "rb1_1", "rb1_2", "rb1_3",
           "ps_bb0", "ps_bb1", "ps_bb2", "ps_bb3",
           P=P, keys=("pbct0", "pbct1", "pbct2", "pbct3",
                      "rb1_0", "rb1_1", "rb1_2", "rb1_3",
                      "ps_bb0", "ps_bb1", "ps_bb2", "ps_bb3"))
    if "bcB" in taps:
        bbt = M.sb("bbt", [128, T], BF16)
        for q in range(2):
            P.dma("sync", bbt[:, q * 1024:(q + 1) * 1024], s_Bb.ap()[0, q],
                  reads=("s_Bb",), writes=("bbt",))
        P.dma("sync", d_taps["bcB"], bbt[:], reads=("bbt",), writes=("tapbcB",))
        M.free("bbt", P=P, keys=("bbt",))

    TH = 1024
    NCH = T // TH
    for dc in range(DC):
        P.dma("sync", s_xc.ap()[:, dc * T:(dc + 1) * T], xin[:, dc * T:(dc + 1) * T],
              reads=("xin",), writes=("s_xc",))
    M.free("xin", P=P, keys=("xin",))

    zeros_t = M.sb("zeros_t", [128, TH], BF16)
    P.op("vector", lambda e: e.memset(zeros_t[:], 0.0), writes=("zeros_t",))
    delta2 = [M.sb(f"delta{i}", [128, T], F32) for i in range(2)]
    ets = [M.sb("et0", [128, TS], F32)] * 2
    sloc2 = [M.sb(f"sloc{i}", [128, T], F32) for i in range(2)]
    u2 = [M.sb(f"u{i}", [128, TH], BF16) for i in range(2)]
    hbuf = M.sb("hbuf", [128, NST * T], BF16)
    da = [M.sb(f"da{i}", [128, TH], BF16) for i in range(3)]
    dec = [M.sb(f"dec{i}", [128, TH], BF16) for i in range(2)]
    dbx = [M.sb(f"dbx{i}", [128, TH], BF16) for i in range(4)]
    bbr = [M.sb(f"bbr{i}", [128, TH], BF16) for i in range(3)]
    cbr = [M.sb(f"cbr{i}", [128, TH], BF16) for i in range(3)]
    xcr = [M.sb(f"xcr{i}", [128, TH], BF16) for i in range(4)]
    gat = [M.sb(f"gat{i}", [128, TH], BF16) for i in range(2)]
    srr = [M.sb(f"srr{i}", [128, TH], BF16) for i in range(2)]
    ya = M.sb("ya", [128, TH], BF16)
    stg2 = [M.sb(f"stg{i}", [128, 2 * NST], F32) for i in range(2)]
    ptmp = M.sb("ptmp", [128, NST], F32)
    g3s = [M.sb(f"g3_{j}", [128, 2 * NST], F32) for j in range(3)]
    gms = M.sb("gms", [128, 3 * 2 * NST], F32)
    ct1 = M.sb("ct1", [128, NST], F32)
    ct2 = M.sb("ct2", [128, NST], F32)
    hin2 = [M.sb(f"hin{i}", [128, NST], F32) for i in range(2)]
    hft = [M.sb(f"hft{i}", [128, TH], BF16) for i in range(2)]
    hfr = [M.sb(f"hfr{i}", [128, TH], BF16) for i in range(2)]
    ymr = [M.sb(f"ymr{i}", [128, TH], BF16) for i in range(2)]
    ybf = M.sb("ybf", [128, TH], BF16)
    yps = M.ps("yps", [128, TH], F32)
    dtps = [M.ps(f"dtps{i}", [128, TS], F32) for i in range(2)]

    def m4_pre(dc):
        pp = dc % 2
        delta, sloc = delta2[pp], sloc2[pp]
        for tc in range(TCN):
            dp_, et_ = dtps[tc % 2], ets[tc % 2]
            P.op("tensor", lambda e, dc=dc, tc=tc, dp_=dp_: e.matmul(
                dp_[:], dtw_b[:, dc * 128:(dc + 1) * 128],
                draw_b[0:NST, tc * TS:(tc + 1) * TS], start=True, stop=True),
                reads=("dtw_b", "draw_b"), writes=(dp_.name,))
            P.op("scalar", lambda e, dc=dc, dp_=dp_, et_=et_: e.activation(
                out=et_[:], in_=dp_[:], func=AF.Exp, bias=col(c_dtb, dc)),
                reads=(dp_.name, "c_dtb"), writes=(et_.name,))
            P.op("scalar", lambda e, tc=tc, et_=et_, delta=delta: e.activation(
                out=delta[:, tc * TS:(tc + 1) * TS], in_=et_[:], func=AF.Ln,
                bias=1.0), reads=(et_.name,), writes=(delta.name,))
        for ch in range(NCH):
            tsl = slice(ch * TH, (ch + 1) * TH)
            P.op("vector", lambda e, ch=ch, tsl=tsl, delta=delta, sloc=sloc:
                 e.tensor_tensor_scan(
                     out=sloc[:, tsl], data0=zeros_t[:], data1=delta[:, tsl],
                     initial=(0.0 if ch == 0 else sloc[:, ch * TH - 1: ch * TH]),
                     op0=ALU.add, op1=ALU.add),
                 reads=("zeros_t", delta.name) + ((sloc.name,) if ch else ()),
                 writes=(sloc.name,))
        if "delta0" in taps and dc == 0:
            P.dma("sync", d_taps["delta0"], delta[:], reads=(delta.name,), writes=("tapd0",))
        if "sloc0" in taps and dc == 0:
            P.dma("sync", d_taps["sloc0"], sloc[:], reads=(sloc.name,), writes=("tapsl",))

    def m4_merged(dcA, dcB):
        if dcA is not None:
            ppA = dcA % 2
            deltaA, slocA, stgA = delta2[ppA], sloc2[ppA], stg2[ppA]
        if dcB is not None:
            ppB = dcB % 2
            slocB, hinB = sloc2[ppB], hin2[ppB]
        for ch in range(NCH):
            tsl = slice(ch * TH, (ch + 1) * TH)
            if dcA is not None:
                xcA = xcr[(dcA % 2) * 2 + ch]
                P.dma("sync", xcA[:],
                      s_xc.ap()[:, dcA * T + ch * TH: dcA * T + (ch + 1) * TH],
                      reads=("s_xc",), writes=(xcA.name,))
                ub_ = u2[ch % 2]
                P.op("vector", lambda e, tsl=tsl, deltaA=deltaA, ub_=ub_, xcA=xcA:
                     e.tensor_tensor(
                         out=ub_[:], in0=deltaA[:, tsl], in1=xcA[:], op=ALU.mult),
                     reads=(deltaA.name, xcA.name), writes=(ub_.name,))
            if dcB is not None:
                sr_ = srr[ch % 2]
                P.dma("sync", sr_[:],
                      s_sres.ap()[:, dcB * T + ch * TH: dcB * T + (ch + 1) * TH],
                      reads=("s_sres",), writes=(sr_.name,))
                xcB = xcr[(dcB % 2) * 2 + ch]
            for n in range(NST):
                if dcB is not None:
                    cb_, de_, hf_ = cbr[n % 3], dec[n % 2], hfr[n % 2]
                    P.dma("scalar", cb_[:], s_Cb.ap()[n, ch], reads=("s_Cb",),
                          writes=(cb_.name,))
                    P.op("scalar", lambda e, dcB=dcB, n=n, de_=de_, tsl=tsl, slocB=slocB:
                         e.activation(
                             out=de_[:], in_=slocB[:, tsl], func=AF.Exp,
                             scale=col(acols, dcB * NST + n)),
                         reads=(slocB.name, "acols"), writes=(de_.name,))
                    hsl = slice(n * T + ch * TH, n * T + (ch + 1) * TH)
                    t1_ = hft[n % 2]
                    P.op("scalar", lambda e, n=n, de_=de_, t1_=t1_, hinB=hinB:
                         e.activation(
                             out=t1_[:], in_=de_[:], func=AF.Identity,
                             scale=col(hinB, n)),
                         reads=(de_.name, hinB.name), writes=(t1_.name,))
                    P.op("vector", lambda e, hf_=hf_, t1_=t1_, hsl=hsl, n=n:
                         e.tensor_tensor(
                             out=hf_[:], in0=t1_[:], in1=hbuf[:, hsl], op=ALU.add),
                         reads=(t1_.name, f"hb{n}"), writes=(hf_.name,))
                    ym_ = ymr[n % 2]
                    P.op("vector", lambda e, hf_=hf_, cb_=cb_, ym_=ym_: e.tensor_tensor(
                        out=ym_[:], in0=hf_[:], in1=cb_[:], op=ALU.mult),
                        reads=(hf_.name, cb_.name), writes=(ym_.name,))

                    def mmacc(e, ym_=ym_, n=n):
                        i = e.matmul(yps[:, 0:TS], eye[:], ym_[:, 0:TS],
                                     start=(n == 0), stop=(n == NST - 1))
                        i = e.matmul(yps[:, TS:TH], eye[:], ym_[:, TS:TH],
                                     start=(n == 0), stop=(n == NST - 1))
                        return i
                    P.op("tensor", mmacc, reads=(ym_.name, "eye"), writes=("yps",))
                if dcA is not None:
                    bb_, da_, dx_ = bbr[n % 3], da[n % 3], dbx[n % 4]
                    P.dma("sync", bb_[:], s_Bb.ap()[n, ch], reads=("s_Bb",),
                          writes=(bb_.name,))
                    P.op("scalar", lambda e, dcA=dcA, n=n, da_=da_, deltaA=deltaA,
                         tsl=tsl: e.activation(
                             out=da_[:], in_=deltaA[:, tsl], func=AF.Exp,
                             scale=col(acols, dcA * NST + n)),
                         reads=(deltaA.name, "acols"), writes=(da_.name,))
                    ub_ = u2[ch % 2]
                    P.op("vector", lambda e, dx_=dx_, bb_=bb_, ub_=ub_:
                         e.tensor_tensor(
                             out=dx_[:], in0=ub_[:], in1=bb_[:], op=ALU.mult),
                         reads=(ub_.name, bb_.name), writes=(dx_.name,))
                    hsl = slice(n * T + ch * TH, n * T + (ch + 1) * TH)
                    P.op("vector", lambda e, hsl=hsl, ch=ch, n=n, da_=da_, dx_=dx_:
                         e.tensor_tensor_scan(
                             out=hbuf[:, hsl], data0=da_[:], data1=dx_[:],
                             initial=(0.0 if ch == 0 else
                                      hbuf[:, n * T + ch * TH - 1: n * T + ch * TH]),
                             op0=ALU.mult, op1=ALU.add),
                         reads=(da_.name, dx_.name, f"hb{n}"), writes=(f"hb{n}",))
                    if ch == NCH - 1:
                        P.op("vector", lambda e, n=n, stgA=stgA: e.tensor_copy(
                            out=col(stgA, n), in_=hbuf[:, n * T + T - 1: n * T + T]),
                            reads=(f"hb{n}",), writes=(stgA.name,))
            if dcB is not None:
                ssl = slice(ch * TH, (ch + 1) * TH)
                P.op("scalar", lambda e: e.activation(
                    out=ybf[:], in_=yps[:], func=AF.Identity),
                    reads=("yps",), writes=("ybf",))
                tds_ = dec[0]
                P.op("vector", lambda e, dcB=dcB, xcB=xcB, tds_=tds_: e.tensor_scalar(
                    out=tds_[:], in0=xcB[:], scalar1=col(c_dsk, dcB), scalar2=None,
                    op0=ALU.mult), reads=(xcB.name, "c_dsk"), writes=(tds_.name,))
                P.op("vector", lambda e, tds_=tds_: e.tensor_tensor(
                    out=ya[:], in0=tds_[:], in1=ybf[:], op=ALU.add),
                    reads=(tds_.name, "ybf"), writes=("ya",))
                if "y0" in taps and dcB == 0:
                    P.dma("sync", d_taps["y0"][:, ssl], ya[:],
                          reads=("ya",), writes=("tapy0",))
                g_ = gat[ch % 2]
                P.op("vector", lambda e, sr_=sr_, g_=g_: e.tensor_tensor(
                    out=g_[:], in0=ya[:], in1=sr_[:], op=ALU.mult),
                    reads=("ya", sr_.name), writes=(g_.name,))
                P.dma("sync", s_xc.ap()[:, dcB * T + ch * TH: dcB * T + (ch + 1) * TH],
                      g_[:], reads=(g_.name,), writes=("s_xc",))
        if dcA is not None and "h0" in taps and dcA == 0:
            P.dma("sync", d_taps["h0"], hbuf[:, 0:T], reads=("hb0",), writes=("taph0",))

    def m4_exchange(dc):
        pp = dc % 2
        sloc, stg, hin = sloc2[pp], stg2[pp], hin2[pp]
        P.op("vector", lambda e, dc=dc, sloc=sloc: e.tensor_scalar(
            out=ptmp[:], in0=acols[:, dc * NST:(dc + 1) * NST],
            scalar1=sloc[:, T - 1:T], scalar2=None, op0=ALU.mult),
            reads=("acols", sloc.name), writes=("ptmp",))
        P.op("scalar", lambda e, stg=stg: e.activation(
            out=stg[:, NST:2 * NST], in_=ptmp[:], func=AF.Exp),
            reads=("ptmp",), writes=(stg.name,))
        P.dma("sync", cin_st[dc], stg[:], reads=(stg.name,), writes=(f"cinst{dc}",))
        P.op("gpsimd", lambda e, dc=dc: e.collective_compute(
            "AllGather", ALU.bypass, ins=[cin_st[dc]], outs=[cout_st[dc].ap()],
            replica_groups=RG), reads=(f"cinst{dc}",), writes=(f"coutst{dc}",))

    def m4_combine(dc):
        hin = hin2[dc % 2]
        dyn_eng = ("gpsimd", "scalar", "sync")[dc % 3]
        for k in (1, 2, 3):
            j = 3 - k

            def dyn_car(e, dc=dc, k=k, j=j, dyn_eng=dyn_eng):
                return e.dma_start(
                    out=g3s[j][:],
                    in_=bass.AP(cout_st[dc].ap().tensor, regs[(dyn_eng, k)],
                                [[2 * NST, 128], [1, 2 * NST]]))
            P.dyn_dma(dyn_eng, dyn_car, reads=(f"coutst{dc}", f"g3_{j}"),
                      writes=(f"g3_{j}",))
        for j in range(3):
            P.op("vector", lambda e, j=j: e.tensor_scalar(
                out=gms[:, j * 32:(j + 1) * 32], in0=g3s[j][:],
                scalar1=col(cmask, 2 - j), scalar2=None, op0=ALU.mult),
                reads=(f"g3_{j}", "cmask"), writes=("gms",))
        P.op("vector", lambda e: e.tensor_tensor(
            out=ct1[:], in0=gms[:, 48:64], in1=gms[:, 0:16], op=ALU.mult),
            reads=("gms",), writes=("ct1",))
        P.op("vector", lambda e: e.tensor_tensor(
            out=ct2[:], in0=gms[:, 32:48], in1=ct1[:], op=ALU.add),
            reads=("gms", "ct1"), writes=("ct2",))
        P.op("vector", lambda e: e.tensor_tensor(
            out=ct1[:], in0=gms[:, 80:96], in1=ct2[:], op=ALU.mult),
            reads=("gms", "ct2"), writes=("ct1",))
        P.op("vector", lambda e, hin=hin: e.tensor_tensor(
            out=hin[:], in0=gms[:, 64:80], in1=ct1[:], op=ALU.add),
            reads=("gms", "ct1"), writes=(hin.name,))

    for dc in range(DC):
        m4_pre(dc)
        if dc > 0:
            m4_combine(dc - 1)
        m4_merged(dc, dc - 1 if dc > 0 else None)
        m4_exchange(dc)
    m4_combine(DC - 1)
    m4_merged(None, DC - 1)

    _m4f = ["zeros_t", "delta0", "delta1", "et0", "sloc0", "sloc1",
            "u0", "u1", "hbuf", "da0", "da1", "da2", "dec0", "dec1",
            "dbx0", "dbx1", "dbx2", "dbx3", "bbr0", "bbr1", "bbr2", "cbr0", "cbr1", "cbr2",
            "xcr0", "xcr1", "xcr2", "xcr3", "gat0", "gat1", "srr0", "srr1", "ya", "ybf", "yps", "hfr0", "hfr1", "ymr0", "ymr1",
            "stg0", "stg1", "ptmp", "g3_0", "g3_1", "g3_2", "gms", "ct1",
            "ct2", "hin0", "hin1", "hft0", "hft1", "dtps0", "dtps1", "draw_b", "xpw_b", "dtw_b", "psp"]
    _m4k = _m4f + [f"hb{n}" for n in range(NST)]
    M.free(*_m4f, P=P, keys=tuple(_m4k))
    if "gated" in taps:
        gt = M.sb("gt", [128, T], BF16)
        for dc in range(DC):
            P.dma("sync", gt[:], s_xc.ap()[:, dc * T:(dc + 1) * T],
                  reads=("s_xc",), writes=("gt",))
            P.dma("sync", d_taps["gated"][:, dc * T:(dc + 1) * T], gt[:],
                  reads=("gt",), writes=("tapg",))
        M.free("gt", P=P, keys=("gt",))

    x2 = M.sb("x2", [128, HC * T], BF16)
    zb1 = M.sb("zb1", [128, HC * T], BF16, side="right")
    woall = M.sb("woall", [128, HC * DC * 128], BF16)
    wof = [M.sb(f"wof{i}", [128, DC * 128], F32, side="right") for i in range(2)]
    for hc in range(HC):
        wf = wof[hc % 2]
        P.dma("gpsimd", wf[:], d_wo[hc], reads=(wf.name,), writes=(wf.name,))
        P.op("vector", lambda e, hc=hc, wf=wf: e.tensor_copy(
            out=woall[:, hc * DC * 128:(hc + 1) * DC * 128], in_=wf[:]),
            reads=(wf.name,), writes=("woall",))
    M.free("wof1", "wof0", P=P, keys=("wof0", "wof1"))
    gtc = [M.sb(f"gtc{i}", [128, DC * TS], BF16) for i in range(2)]
    x1r = [M.sb(f"x1r{i}", [128, TS], BF16) for i in range(2)]
    pso = [M.ps(f"pso{i}", [128, TS], F32) for i in range(2)]
    for tc in range(TCN):
        gb = gtc[tc % 2]
        for dc in range(DC):
            P.dma("gpsimd", gb[:, dc * TS:(dc + 1) * TS],
                  s_xc.ap()[:, dc * T + tc * TS: dc * T + (tc + 1) * TS],
                  reads=("s_xc",), writes=(gb.name,))
        for hc in range(HC):
            x1_ = x1r[hc % 2]
            P.dma("sync", x1_[:],
                  s_x1.ap()[:, hc * T + tc * TS: hc * T + (tc + 1) * TS],
                  reads=("s_x1",), writes=(x1_.name,))
            po_ = pso[hc % 2]

            def mmo(e, hc=hc, po_=po_, gb=gb):
                for dc in range(DC):
                    i = e.matmul(po_[:], woall[:, hc * DC * 128 + dc * 128:
                                               hc * DC * 128 + (dc + 1) * 128],
                                 gb[:, dc * TS:(dc + 1) * TS],
                                 start=(dc == 0), stop=(dc == DC - 1))
                return i
            P.op("tensor", mmo, reads=("woall", gb.name), writes=(po_.name,))
            P.op("vector", lambda e, hc=hc, tc=tc, x1_=x1_, po_=po_: e.scalar_tensor_tensor(
                out=x2[:, hc * T + tc * TS: hc * T + (tc + 1) * TS], in0=po_[:],
                scalar=col(c_bo, hc), in1=x1_[:], op0=ALU.add, op1=ALU.add),
                reads=(po_.name, "c_bo", x1_.name), writes=("x2",))
        ln_tc(tc, x2, "x2", zb1, "zb1", True)
    M.free("woall", "gtc0", "gtc1", "x1r0", "x1r1", "pso0", "pso1", P=P,
           keys=("woall", "gtc0", "gtc1", "x1r0", "x1r1", "pso0", "pso1"))
    if "x2" in taps:
        P.dma("sync", d_taps["x2"], x2[:], reads=("x2",), writes=("tapx2",))

    ffn(1, x2, "x2", None, None, final=True, zb_pre=zb1)

    P.barrier_wait("sync", ("y_out",))

    with nc.Block() as block:
        P.finalize(block)
    es.close()
    return nc


_NC_CACHE = {}


def _get_nc(taps=()):
    key = tuple(taps)
    if key not in _NC_CACHE:
        _NC_CACHE[key] = build_program(taps)
    return _NC_CACHE[key]


def _stage_inputs(inputs, taps=()):
    f32 = np.float32
    bf = ml_dtypes.bfloat16
    w_in = np.asarray(inputs["L0_attn_in_proj_w"], f32)
    w_out = np.asarray(inputs["L0_attn_out_proj_w"], f32)
    com = {
        "w_in_t": np.ascontiguousarray(
            w_in.reshape(HC, 128, 2 * DC, 128).transpose(2, 1, 0, 3)
            .reshape(2 * DC, 128, HC * 128)),
        "w_out_t": np.ascontiguousarray(
            w_out.reshape(DC, 128, HC, 128).transpose(2, 1, 0, 3)
            .reshape(HC, 128, DC * 128)),
        "w_gate": np.ascontiguousarray(np.asarray(inputs["L0_ffn_w_gate"], f32)),
        "w_up": np.ascontiguousarray(np.asarray(inputs["L0_ffn_w_up"], f32)),
        "w_down": np.ascontiguousarray(np.asarray(inputs["L0_ffn_w_down"], f32)),
        "x_proj_w": np.ascontiguousarray(np.asarray(inputs["L0_attn_x_proj_w"], f32)),
        "dt_proj_w": np.ascontiguousarray(np.asarray(inputs["L0_attn_dt_proj_w"], f32)),
        "b_in_t": np.ascontiguousarray(
            np.asarray(inputs["L0_attn_in_proj_b"], f32).reshape(2 * DC, 128).T),
        "b_out_t": np.ascontiguousarray(
            np.asarray(inputs["L0_attn_out_proj_b"], f32).reshape(HC, 128).T),
        "x_proj_b3": np.ascontiguousarray(
            np.asarray(inputs["L0_attn_x_proj_b"], f32).reshape(3, NST).T),
        "dt_b_t": np.ascontiguousarray(
            np.asarray(inputs["L0_attn_dt_proj_b"], f32).reshape(DC, 128).T),
        "conv_w_t": np.ascontiguousarray(
            np.asarray(inputs["L0_attn_conv_w"], f32).reshape(4, DC, 128)
            .transpose(2, 1, 0).reshape(128, DC * 4)),
        "conv_b_t": np.ascontiguousarray(
            np.asarray(inputs["L0_attn_conv_b"], f32).reshape(DC, 128).T),
        "a_log_t": np.ascontiguousarray(
            np.asarray(inputs["L0_attn_A_log"], f32).reshape(DC, 128, NST)
            .transpose(1, 0, 2).reshape(128, DC * NST)),
        "d_skip_t": np.ascontiguousarray(
            np.asarray(inputs["L0_attn_D_skip"], f32).reshape(DC, 128).T),
        "clip_x": np.asarray(inputs["L0_ffn_clip_x"], f32).reshape(1, 1),
        "clip_down": np.asarray(inputs["L0_ffn_clip_down"], f32).reshape(1, 1),
        "eye128": np.eye(128, dtype=bf),
        "ones1x128": np.ones((1, 128), dtype=bf),
        "ones1f": np.ones((1, 128), dtype=f32),
    }
    x = np.asarray(inputs["L0_attn_x"], f32)
    xs = x.reshape(NCORES, T, H)
    in_maps = []
    for c in range(NCORES):
        p = c % CHAIN
        im = dict(com)
        im["x_t"] = np.ascontiguousarray(xs[c].T)
        im["rank"] = np.array([[c]], dtype=np.int32)
        msk = np.zeros((128, 3), f32)
        for k in range(3):
            if p >= k + 1:
                msk[:, k] = 1.0
        im["chain_masks"] = msk
        bsel = np.zeros((128, 8), f32)
        if p >= 1:
            bsel[:, c - 1] = 1.0
        im["bnd_sel"] = bsel
        in_maps.append(im)
    return in_maps


def kernel(**inputs):
    taps = tuple(_TAPS)
    nc = _get_nc(taps)
    in_maps = _stage_inputs(inputs, taps)
    res = run_bass_kernel_spmd(nc, in_maps, core_ids=list(range(NCORES)))
    out = np.empty((B, S, H), np.float32)
    outs = out.reshape(NCORES, T, H)
    for c in range(NCORES):
        outs[c] = res.results[c]["y"].T
    if taps:
        kernel._tap_results = [{k: r[f"tap_{k}"] for k in taps} for r in res.results]
    return out


# version 30
# speedup vs baseline: 1.0072x; 1.0072x over previous
import numpy as np
import ml_dtypes
from contextlib import ExitStack

import concourse.bass as bass
import concourse.mybir as mybir
from concourse.bass_utils import run_bass_kernel_spmd

F32 = mybir.dt.float32
BF16 = mybir.dt.bfloat16
I16 = mybir.dt.int16
I32 = mybir.dt.int32
ALU = mybir.AluOpType
AF = mybir.ActivationFunctionType
AX = mybir.AxisListType

B, S, H = 2, 8192, 1024
NCORES = 8
T = (B * S) // NCORES
TS = 512
TCN = T // TS
HC = H // 128
I_FFN = 2816
IC = I_FFN // 128
D_IN = 2048
DC = D_IN // 128
NST = 16
CHAIN = 4
QMAX = 127.0

ENGINES = ("sync", "scalar", "vector", "gpsimd", "tensor")

_TAPS = ()


class PK:

    DMA_SLOTS = 6

    def __init__(self, nc, es):
        self.nc = nc
        self.q = {e: [] for e in ENGINES}
        self.esem = {e: es.enter_context(nc.semaphore(f"es_{e}")) for e in ENGINES}
        self.dsem = {e: [es.enter_context(nc.semaphore(f"ds_{e}{i}"))
                         for i in range(self.DMA_SLOTS)] for e in ENGINES}
        self.ecnt = {e: 0 for e in ENGINES}
        self.dcnt = {e: [0] * self.DMA_SLOTS for e in ENGINES}
        self.dslot = {e: 0 for e in ENGINES}
        self.reg = {}
        self.waited = {e: {} for e in ENGINES}

    def _r(self, key):
        if key not in self.reg:
            self.reg[key] = {"w": None, "r": []}
        return self.reg[key]

    def _wants(self, eng, reads, writes):
        toks = []
        for k in reads:
            r = self._r(k)
            if r["w"] is not None:
                toks.append(r["w"])
        for k in writes:
            r = self._r(k)
            if r["w"] is not None:
                toks.append(r["w"])
            toks.extend(r["r"])
        best = {}
        for sem, v in toks:
            if best.get(sem.name, (None, -1))[1] < v:
                best[sem.name] = (sem, v)
        out = []
        for sem, v in best.values():
            if self.waited[eng].get(sem.name, -1) < v:
                out.append((sem, v))
                self.waited[eng][sem.name] = v
        return out

    def _update(self, tok, reads, writes):
        for k in reads:
            self._r(k)["r"].append(tok)
        for k in writes:
            r = self._r(k)
            r["w"] = tok
            r["r"] = []

    def _fresh_mem_barrier(self, eng, writes):
        fresh = [k for k in writes if k not in self.reg]
        if fresh and "__mem__" in self.reg:
            return self._wants(eng, (), ("__mem__",))
        return []

    def note_freed(self, keys):
        m = self._r("__mem__")
        for k in keys:
            r = self.reg.get(k)
            if r is None:
                continue
            if r["w"] is not None:
                m["r"].append(r["w"])
            m["r"].extend(r["r"])

    def op(self, eng, fn, reads=(), writes=(), inc=True):
        waits = self._fresh_mem_barrier(eng, writes) + self._wants(eng, reads, writes)
        sem = self.esem[eng]
        if inc:
            self.ecnt[eng] += 1
            tok = (sem, self.ecnt[eng])
            self._update(tok, reads, writes)
        val = self.ecnt[eng]

        def emit(e, fn=fn, waits=waits, inc=inc, sem=sem, val=val):
            for s, v in waits:
                e.wait_ge(s, v)
            inst = fn(e)
            if inc:
                inst.then_inc(sem, 1)
        self.q[eng].append(emit)

    def _dma_slot(self, eng):
        slot = self.dslot[eng]
        self.dslot[eng] = (slot + 1) % self.DMA_SLOTS
        sem = self.dsem[eng][slot]
        prev = self.dcnt[eng][slot]
        self.dcnt[eng][slot] = prev + 16
        extra = []
        if prev > 0 and self.waited[eng].get(sem.name, -1) < prev:
            extra.append((sem, prev))
            self.waited[eng][sem.name] = prev
        return sem, prev + 16, extra

    def dma(self, eng, out_ap, in_ap, reads=(), writes=()):
        waits = self._fresh_mem_barrier(eng, writes) + self._wants(eng, reads, writes)
        sem, val, extra = self._dma_slot(eng)
        waits = extra + waits
        tok = (sem, val)
        self._update(tok, reads, writes)

        def emit(e, out_ap=out_ap, in_ap=in_ap, waits=waits, sem=sem):
            for s, v in waits:
                e.wait_ge(s, v)
            oa = out_ap() if callable(out_ap) else out_ap
            ia = in_ap() if callable(in_ap) else in_ap
            e.dma_start(out=oa, in_=ia).then_inc(sem, 16)
        self.q[eng].append(emit)

    def dyn_dma(self, eng, fn, reads=(), writes=()):
        waits = self._fresh_mem_barrier(eng, writes) + self._wants(eng, reads, writes)
        sem, val, extra = self._dma_slot(eng)
        waits = extra + waits
        tok = (sem, val)
        self._update(tok, reads, writes)

        def emit(e, fn=fn, waits=waits, sem=sem):
            for s, v in waits:
                e.wait_ge(s, v)
            fn(e).then_inc(sem, 16)
        self.q[eng].append(emit)

    def raw(self, eng, fn):
        self.q[eng].append(lambda e, fn=fn: fn(e))

    def barrier_wait(self, eng, keys):
        waits = self._wants(eng, keys, ())

        def emit(e, waits=waits):
            for s, v in waits:
                e.wait_ge(s, v)
        self.q[eng].append(emit)

    def finalize(self, block):
        for e in ENGINES:
            getattr(block, e)(lambda eng, q=tuple(self.q[e]): [f(eng) for f in q])


class Mem:
    def __init__(self, nc):
        self.nc = nc
        self.guards = {}
        self.seq = 0

    def sb(self, name, shape, dtype, side="left"):
        g = self.nc.sbuf_tensor(name, shape, dtype, side=side)
        h = g.__enter__()
        self.seq += 1
        self.guards[name] = (g, self.seq)
        return h

    def ps(self, name, shape, dtype):
        g = self.nc.psum_tensor(shape, dtype)
        h = g.__enter__()
        self.seq += 1
        self.guards[name] = (g, self.seq)
        return h

    def free(self, *names, P=None, keys=()):
        if P is not None:
            P.note_freed(keys if keys else names)
        for n in sorted(names, key=lambda n: -self.guards[n][1]):
            self.guards.pop(n)[0].__exit__(None, None, None)


def build_program(taps=()):
    nc = bass.Bass("TRN2", target_bir_lowering=False)
    es = ExitStack()

    d_x = nc.dram_tensor("x_t", [H, T], F32, kind="ExternalInput").ap()
    d_win = nc.dram_tensor("w_in_t", [2 * DC, 128, HC * 128], F32, kind="ExternalInput").ap()
    d_wo = nc.dram_tensor("w_out_t", [HC, 128, DC * 128], F32, kind="ExternalInput").ap()
    d_wg = nc.dram_tensor("w_gate", [I_FFN, H], F32, kind="ExternalInput").ap()
    d_wu = nc.dram_tensor("w_up", [I_FFN, H], F32, kind="ExternalInput").ap()
    d_wd = nc.dram_tensor("w_down", [H, I_FFN], F32, kind="ExternalInput").ap()
    d_xpw = nc.dram_tensor("x_proj_w", [D_IN, 48], F32, kind="ExternalInput").ap()
    d_dtw = nc.dram_tensor("dt_proj_w", [NST, D_IN], F32, kind="ExternalInput").ap()
    d_bin = nc.dram_tensor("b_in_t", [128, 2 * DC], F32, kind="ExternalInput").ap()
    d_bo = nc.dram_tensor("b_out_t", [128, HC], F32, kind="ExternalInput").ap()
    d_xpb = nc.dram_tensor("x_proj_b3", [NST, 3], F32, kind="ExternalInput").ap()
    d_dtb = nc.dram_tensor("dt_b_t", [128, DC], F32, kind="ExternalInput").ap()
    d_cw = nc.dram_tensor("conv_w_t", [128, DC * 4], F32, kind="ExternalInput").ap()
    d_cb = nc.dram_tensor("conv_b_t", [128, DC], F32, kind="ExternalInput").ap()
    d_alog = nc.dram_tensor("a_log_t", [128, DC * NST], F32, kind="ExternalInput").ap()
    d_dsk = nc.dram_tensor("d_skip_t", [128, DC], F32, kind="ExternalInput").ap()
    d_clx = nc.dram_tensor("clip_x", [1, 1], F32, kind="ExternalInput").ap()
    d_cld = nc.dram_tensor("clip_down", [1, 1], F32, kind="ExternalInput").ap()
    d_eye = nc.dram_tensor("eye128", [128, 128], BF16, kind="ExternalInput").ap()
    d_ones1 = nc.dram_tensor("ones1x128", [1, 128], BF16, kind="ExternalInput").ap()
    d_ones1f = nc.dram_tensor("ones1f", [1, 128], F32, kind="ExternalInput").ap()
    d_rank = nc.dram_tensor("rank", [1, 1], I32, kind="ExternalInput").ap()
    d_msk = nc.dram_tensor("chain_masks", [128, 3], F32, kind="ExternalInput").ap()
    d_bsel = nc.dram_tensor("bnd_sel", [128, 8], F32, kind="ExternalInput").ap()

    d_y = nc.dram_tensor("y", [H, T], F32, kind="ExternalOutput").ap()

    s_wgT = nc.dram_tensor("s_wgT", [IC, 128, HC * 128], BF16, kind="Internal")
    s_wuT = nc.dram_tensor("s_wuT", [IC, 128, HC * 128], BF16, kind="Internal")
    s_wdT = nc.dram_tensor("s_wdT", [HC, 128, IC * 128], BF16, kind="Internal")
    s_x1 = nc.dram_tensor("s_x1", [128, HC * T], BF16, kind="Internal")
    s_xr = nc.dram_tensor("s_xr", [128, HC * T], BF16, kind="Internal")
    s_sres = nc.dram_tensor("s_sres", [128, DC * T], BF16, kind="Internal")
    s_xc = nc.dram_tensor("s_xc", [128, DC * T], BF16, kind="Internal")
    s_ab = nc.dram_tensor("s_ab", [128, IC * T], BF16, kind="Internal")
    s_bc = nc.dram_tensor("s_bc", [2, NST, T], BF16, kind="Internal")
    s_Bb = nc.dram_tensor("s_Bb", [NST, 2, 128, 1024], BF16, kind="Internal")
    s_Cb = nc.dram_tensor("s_Cb", [NST, 2, 128, 1024], BF16, kind="Internal")
    cin_mx = [nc.dram_tensor(f"cin_mx{f}", [1, 1], F32, kind="Internal").ap() for f in range(4)]
    cout_mx = [nc.dram_tensor(f"cout_mx{f}", [1, 1], F32, kind="Internal").ap() for f in range(4)]
    cin_bnd = nc.dram_tensor("cin_bnd", [128, DC * 3], BF16, kind="Internal").ap()
    cout_bnd = nc.dram_tensor("cout_bnd", [8, 128, DC * 3], BF16, kind="Internal").ap()
    cin_st = [nc.dram_tensor(f"cin_st{d}", [128, 2 * NST], F32, kind="Internal").ap() for d in range(DC)]
    cout_st = [nc.dram_tensor(f"cout_st{d}", [8, 128, 2 * NST], F32, kind="Internal") for d in range(DC)]
    _tapspec = {
        "z1": (HC * T, BF16), "xq1": (HC * T, BF16), "a1": (IC * T, BF16),
        "x1": (HC * T, BF16), "xin": (DC * T, BF16), "xc": (DC * T, BF16),
        "bcB": (T, BF16), "delta0": (T, F32), "h0": (T, BF16),
        "gated": (DC * T, BF16), "x2": (HC * T, BF16), "sres": (DC * T, BF16),
        "y0": (T, F32), "sloc0": (T, F32),
    }
    d_taps = {t: nc.dram_tensor(f"tap_{t}", [128, _tapspec[t][0]], _tapspec[t][1],
                                kind="ExternalOutput").ap() for t in taps}

    P = PK(nc, es)
    M = Mem(nc)
    RG = [list(range(NCORES))]

    onesc = M.sb("onesc", [128, 1], BF16)
    ones1 = M.sb("ones1", [1, 128], BF16)
    ones1f = M.sb("ones1f_sb", [1, 128], F32)
    eye = M.sb("eye", [128, 128], BF16)
    c_bin = M.sb("c_bin", [128, 2 * DC], F32)
    c_bo = M.sb("c_bo", [128, HC], F32)
    c_xpb3 = M.sb("c_xpb3", [NST, 3], F32)
    c_dtb = M.sb("c_dtb", [128, DC], F32)
    c_cw = M.sb("c_cw", [128, DC * 4], F32)
    c_cb = M.sb("c_cb", [128, DC], F32)
    c_dsk = M.sb("c_dsk", [128, DC], F32)
    acols = M.sb("acols", [128, DC * NST], F32)
    sgr = M.sb("sgr", [128, IC], F32)
    sur = M.sb("sur", [128, IC], F32)
    sdr = M.sb("sdr", [128, HC], F32)
    sig_x = M.sb("sig_x", [1, 1], F32)
    sig_h = M.sb("sig_h", [1, 1], F32)
    rankb = M.sb("rankb", [1, 1], I32)
    cmask = M.sb("cmask", [128, 3], F32)
    bselc = M.sb("bselc", [128, 8], F32)
    smtmp = M.sb("smtmp", [1, 4], F32)
    qcols = M.sb("qcols", [128, 16], F32)
    sgx = M.sb("sgx", [2, 128, IC], F32) if False else M.sb("sgx", [128, 2 * IC], F32)
    sux = M.sb("sux", [128, 2 * IC], F32)
    sdh = M.sb("sdh", [128, 2 * HC], F32)

    def col(t, j):
        return t[:, j:j + 1]

    for dst, src in ((c_bin, d_bin), (c_bo, d_bo), (c_xpb3, d_xpb), (c_dtb, d_dtb),
                     (c_cw, d_cw), (c_cb, d_cb), (c_dsk, d_dsk),
                     (eye, d_eye), (ones1, d_ones1), (ones1f, d_ones1f), (rankb, d_rank),
                     (cmask, d_msk), (bselc, d_bsel)):
        P.dma("sync", dst[:], src, writes=(dst.name,))
    P.dma("sync", acols[:], d_alog, writes=("acols",))
    P.op("vector", lambda e: e.memset(onesc[:], 1.0), writes=("onesc",))
    atmp = M.sb("atmp", [128, DC * NST], F32)
    P.op("scalar", lambda e: e.activation(out=atmp[:], in_=acols[:], func=AF.Exp),
         reads=("acols",), writes=("atmp",))
    P.op("vector", lambda e: e.tensor_scalar(out=acols[:], in0=atmp[:], scalar1=-1.0,
                                             scalar2=None, op0=ALU.mult),
         reads=("atmp",), writes=("acols",))
    M.free("atmp", P=P, keys=("atmp",))
    P.dma("sync", smtmp[:, 0:1], d_clx, writes=("smtmp0",))
    P.dma("sync", smtmp[:, 1:2], d_cld, writes=("smtmp1",))
    P.op("scalar", lambda e: e.activation(out=sig_x[:], in_=smtmp[:, 0:1], func=AF.Sigmoid),
         reads=("smtmp0",), writes=("sig_x",))
    P.op("scalar", lambda e: e.activation(out=sig_h[:], in_=smtmp[:, 1:2], func=AF.Sigmoid),
         reads=("smtmp1",), writes=("sig_h",))

    regs = {}

    def setup_regs(eng_name):
        def inner(g):
            for k in (1, 2, 3):
                rk = g.alloc_register(f"r_car{k}_{eng_name}")
                g.reg_load(rk, rankb[:1, :1])
                g.reg_add(rk, rk, 8 - k)
                g.reg_mod(rk, rk, 8)
                g.reg_mul(rk, rk, 128 * 2 * NST)
                regs[(eng_name, k)] = rk
            if eng_name == "gpsimd":
                r2 = g.alloc_register("r_bnd")
                g.reg_load(r2, rankb[:1, :1])
                g.reg_add(r2, r2, 7)
                g.reg_mod(r2, r2, 8)
                g.reg_mul(r2, r2, 128 * DC * 3)
                regs["bnd"] = r2
        return inner
    for _en in ("gpsimd", "scalar", "sync"):
        P.barrier_wait(_en, ("rankb",))
        P.raw(_en, setup_regs(_en))


    def quant_weight(src_ap, n_rows_tiles, in_dim, scale_buf, scratch, tag, base=0, q="sync"):
        in_c = in_dim // 128
        for rt in range(n_rows_tiles):
            pp = (base + rt) % 2
            wst_, wqi_, wqb_ = wst[pp], wqi[pp], wqb[pp]
            rmax_, rtmp_, rrec_ = rmax[pp], rtmp[pp], rrec[pp]
            P.dma(q, wst_[:, 0:in_dim], src_ap[rt * 128:(rt + 1) * 128, :],
                  writes=(wst_.name,))
            P.op("vector", lambda e, w=in_dim, wst_=wst_, rmax_=rmax_: e.tensor_reduce(
                out=rmax_[:], in_=wst_[:, 0:w], axis=AX.X, op=ALU.max,
                apply_absolute_value=True), reads=(wst_.name,), writes=(rmax_.name,))
            P.op("vector", lambda e, rmax_=rmax_, rtmp_=rtmp_: e.reciprocal(
                out=rtmp_[:], in_=rmax_[:]),
                reads=(rmax_.name,), writes=(rtmp_.name,))
            P.op("vector", lambda e, rtmp_=rtmp_, rrec_=rrec_: e.tensor_scalar(
                out=rrec_[:], in0=rtmp_[:], scalar1=QMAX, scalar2=None, op0=ALU.mult),
                reads=(rtmp_.name,), writes=(rrec_.name,))
            P.op("vector", lambda e, rt=rt, rmax_=rmax_: e.tensor_scalar(
                out=col(scale_buf, rt), in0=rmax_[:], scalar1=1.0 / QMAX, scalar2=None,
                op0=ALU.mult), reads=(rmax_.name,), writes=(scale_buf.name,))
            P.op("scalar", lambda e, w=in_dim, wst_=wst_, wqi_=wqi_, rrec_=rrec_: e.activation(
                out=wqi_[:, 0:w], in_=wst_[:, 0:w], func=AF.Copy, scale=rrec_[:]),
                reads=(wst_.name, rrec_.name), writes=(wqi_.name,))
            P.op("vector", lambda e, w=in_dim, wqi_=wqi_, wqb_=wqb_: e.tensor_copy(
                out=wqb_[:, 0:w], in_=wqi_[:, 0:w]),
                reads=(wqi_.name,), writes=(wqb_.name,))
            for kb in range(0, in_c, 8):
                nb = min(8, in_c - kb)
                pt_, wtr_ = ps_tr[pp], wtr[pp]

                def trans(e, kb=kb, nb=nb, wqb_=wqb_, pt_=pt_):
                    for j in range(nb):
                        inst = e.transpose(pt_[:, j * 128:(j + 1) * 128],
                                           wqb_[:, (kb + j) * 128:(kb + j + 1) * 128], eye[:])
                    return inst
                P.op("tensor", trans, reads=(wqb_.name, "eye"), writes=(pt_.name,))
                P.op("vector", lambda e, nb=nb, pt_=pt_, wtr_=wtr_: e.tensor_copy(
                    out=wtr_[:, 0:nb * 128], in_=pt_[:, 0:nb * 128]),
                    reads=(pt_.name,), writes=(wtr_.name,))
                P.dma("sync",
                      scratch.ap()[rt, :, kb * 128:(kb + nb) * 128],
                      wtr_[:, 0:nb * 128],
                      reads=(wtr_.name,), writes=(f"scr_{tag}{rt}",))
    def run_w1d():
        _sub = [f"{n}{i}{k}" for n in ("wst", "wqi", "wqb") for i in range(2)
                for k in ("g", "u")]
        P.barrier_wait("gpsimd", tuple(_sub))
        quant_weight(d_wd, HC, I_FFN, sdr, s_wdT, "wd", q="gpsimd")
        _w1 = [f"{n}{i}" for n in ("wst", "wqi", "wqb", "wtr", "rmax", "rtmp",
                                   "rrec", "umax", "utmp", "urec", "ps_tr")
               for i in range(2)]
        M.free(*_w1, P=P, keys=tuple(_w1) + tuple(_sub))


    sqtA = M.sb("sqtA", [128, TS], BF16)
    sqtB = M.sb("sqtB", [128, TS], BF16)
    rowt = [M.sb(f"rowt{j}", [1, TS], F32) for j in range(7)]
    rowsb0 = M.sb("rowsb0", [1, TS], BF16)
    rowsb1 = M.sb("rowsb1", [1, TS], BF16)
    zmx = M.sb("zmx", [128, 40], F32)
    amx = M.sb("amx", [128, TCN * IC], F32)
    zred = M.sb("zred", [128, 1], F32)
    gq = M.sb("gq", [1, 1], F32)
    redrow = M.sb("redrow", [1, 128], F32)
    sct = M.sb("sct", [1, 4], F32)
    scc = M.sb("scc", [128, 4], F32)
    i16t = [M.sb(f"i16t_{i}", [128, TS], I16) for i in range(4)]
    tz1 = [M.sb(f"tz1_{i}", [128, TS], BF16) for i in range(2)]
    rbsb = M.sb("rbsb", [128, TS], BF16)
    mbsb = M.sb("mbsb", [128, TS], BF16)
    ps_stat = M.ps("ps_stat", [33, TS], F32)
    ps_bc = M.ps("ps_bc", [128, TS], F32)

    def r_sl(j, tc):
        return rowt[j][0:1, 0:TS]

    def rb_sl(j, tc):
        t_ = rowsb0 if j == 0 else rowsb1
        return t_[0:1, 0:TS]

    def ln_stats_and_z(xt, xkey, zb, zkey, do_max, tag):
        for tc in range(TCN):
            ln_tc(tc, xt, xkey, zb, zkey, do_max)

    def ln_tc(tc, xt, xkey, zb, zkey, do_max):
        if callable(xkey):
            xkey = xkey(tc)
        if True:
            xm = lambda hc, tc=tc: xt[:, hc * T + tc * TS: hc * T + (tc + 1) * TS]

            def mm_sx(e, tc=tc, xm=xm):
                for hc in range(HC):
                    i = e.matmul(ps_stat[0:1, :], onesc[:], xm(hc), start=(hc == 0), stop=(hc == HC - 1))
                return i
            P.op("tensor", mm_sx, reads=(xkey, "onesc"), writes=("ps_sx",))
            for hc in range(HC):
                sq = sqtA if hc % 2 == 0 else sqtB
                P.op("scalar", lambda e, hc=hc, sq=sq, xm=xm: e.activation(
                    out=sq[:], in_=xm(hc), func=AF.Square),
                    reads=(xkey,), writes=(sq.name,))
                P.op("tensor", lambda e, hc=hc, sq=sq: e.matmul(
                    ps_stat[32:33, :], onesc[:], sq[:], start=(hc == 0), stop=(hc == HC - 1)),
                    reads=(sq.name, "onesc"), writes=("ps_sxx",))
            P.op("vector", lambda e, tc=tc: e.tensor_scalar(
                out=r_sl(0, tc), in0=ps_stat[0:1, :], scalar1=1.0 / H, scalar2=None, op0=ALU.mult),
                reads=("ps_sx",), writes=("rows0",))
            P.op("vector", lambda e, tc=tc: e.tensor_scalar(
                out=r_sl(2, tc), in0=ps_stat[32:33, :], scalar1=1.0 / H, scalar2=None, op0=ALU.mult),
                reads=("ps_sxx",), writes=("rows2",))
            P.op("vector", lambda e, tc=tc: e.tensor_tensor(
                out=r_sl(3, tc), in0=r_sl(0, tc), in1=r_sl(0, tc), op=ALU.mult),
                reads=("rows0",), writes=("rows3",))
            P.op("vector", lambda e, tc=tc: e.scalar_tensor_tensor(
                out=r_sl(4, tc), in0=r_sl(2, tc), scalar=1e-5, in1=r_sl(3, tc),
                op0=ALU.add, op1=ALU.subtract),
                reads=("rows2", "rows3"), writes=("rows4",))
            P.op("scalar", lambda e, tc=tc: e.activation(
                out=r_sl(6, tc), in_=r_sl(4, tc), func=AF.Sqrt),
                reads=("rows4",), writes=("rows6",))
            P.op("vector", lambda e, tc=tc: e.reciprocal(
                out=r_sl(1, tc), in_=r_sl(6, tc)),
                reads=("rows6",), writes=("rows1",))
            P.op("vector", lambda e, tc=tc: e.tensor_tensor(
                out=r_sl(5, tc), in0=r_sl(0, tc), in1=r_sl(1, tc), op=ALU.mult),
                reads=("rows0", "rows1"), writes=("rows5",))
            P.op("vector", lambda e, tc=tc: e.tensor_copy(
                out=rb_sl(0, tc), in_=r_sl(1, tc)), reads=("rows1",), writes=("rowsb0",))
            P.op("vector", lambda e, tc=tc: e.tensor_copy(
                out=rb_sl(1, tc), in_=r_sl(5, tc)), reads=("rows5",), writes=("rowsb1",))
            P.op("tensor", lambda e, tc=tc: e.matmul(
                ps_bc[:], ones1[:], rb_sl(0, tc), start=True, stop=True),
                reads=("rowsb0", "ones1"), writes=("ps_bc",))
            P.op("vector", lambda e: e.tensor_copy(out=rbsb[:], in_=ps_bc[:]),
                 reads=("ps_bc",), writes=("rbsb",))
            P.op("tensor", lambda e, tc=tc: e.matmul(
                ps_bc[:], ones1[:], rb_sl(1, tc), start=True, stop=True),
                reads=("rowsb1", "ones1"), writes=("ps_bc",))
            P.op("vector", lambda e: e.tensor_copy(out=mbsb[:], in_=ps_bc[:]),
                 reads=("ps_bc",), writes=("mbsb",))
            for hc in range(HC):
                t_ = tz1[hc % 2]
                P.op("vector", lambda e, hc=hc, xm=xm, t_=t_: e.tensor_tensor(
                    out=t_[:], in0=xm(hc), in1=rbsb[:], op=ALU.mult),
                    reads=(xkey, "rbsb"), writes=(t_.name,))
                P.op("vector", lambda e, hc=hc, tc=tc, t_=t_: e.tensor_tensor(
                    out=zb[:, hc * T + tc * TS: hc * T + (tc + 1) * TS],
                    in0=t_[:], in1=mbsb[:], op=ALU.subtract),
                    reads=(t_.name, "mbsb"), writes=(zkey,))
                if do_max:
                    P.op("vector", lambda e, hc=hc, tc=tc: e.tensor_reduce(
                        out=col(zmx, tc * HC + hc),
                        in_=zb[:, hc * T + tc * TS: hc * T + (tc + 1) * TS],
                        axis=AX.X, op=ALU.max, apply_absolute_value=True),
                        reads=(zkey,), writes=("zmx",))


    def part_max_to_sct3(src_key):
        P.dma("sync", redrow[0:1, 0:128], zred[:, 0:1], reads=(src_key,), writes=("redrow",))
        P.op("vector", lambda e: e.tensor_reduce(
            out=sct[0:1, 3:4], in_=redrow[:], axis=AX.X, op=ALU.max),
            reads=("redrow",), writes=("sct3",))

    def col_bcast(dst_col, dst_key, src_sl, src_key, pcol):
        P.op("tensor", lambda e: e.matmul(
            ps_bc[:, pcol:pcol + 1], ones1f[:], src_sl, start=True, stop=True),
            reads=(src_key, "ones1f"), writes=(f"ps_bc{pcol}",))
        P.op("vector", lambda e: e.tensor_copy(out=dst_col, in_=ps_bc[:, pcol:pcol + 1]),
             reads=(f"ps_bc{pcol}",), writes=(dst_key,))

    def global_max_scale(idx, n_cols, src_key, sig, tag, mid=None):
        P.op("vector", lambda e: e.tensor_reduce(
            out=zred[:], in_=zmx[:, 0:n_cols], axis=AX.X, op=ALU.max),
            reads=(src_key,), writes=("zred",))
        part_max_to_sct3("zred")
        P.dma("sync", cin_mx[idx], sct[0:1, 3:4], reads=("sct3",), writes=(f"cinmx{idx}",))
        P.op("gpsimd", lambda e: e.collective_compute(
            "AllReduce", ALU.max, ins=[cin_mx[idx]], outs=[cout_mx[idx]],
            replica_groups=RG), reads=(f"cinmx{idx}",), writes=(f"coutmx{idx}",))
        if mid is not None:
            mid()
        P.dma("sync", gq[:], cout_mx[idx], reads=(f"coutmx{idx}",), writes=("gq",))
        P.op("vector", lambda e, sig=sig: e.tensor_tensor(
            out=sct[0:1, 0:1], in0=gq[:], in1=sig[:], op=ALU.mult),
            reads=("gq", sig.name), writes=("sct0",))
        P.op("vector", lambda e: e.tensor_scalar(
            out=sct[0:1, 1:2], in0=sct[0:1, 0:1], scalar1=1.0 / QMAX, scalar2=None,
            op0=ALU.mult), reads=("sct0",), writes=("sct1",))
        P.op("vector", lambda e: e.reciprocal(
            out=sct[0:1, 2:3], in_=sct[0:1, 1:2]),
            reads=("sct1",), writes=("sct2",))
        col_bcast(col(scc, 0), "scc0", sct[0:1, 1:2], "sct1", 0)
        col_bcast(col(scc, 1), "scc1", sct[0:1, 2:3], "sct2", 1)

    def ffn(f, xt, xkey, x_new, x_new_key, final, zb_pre=None):
        zkey = f"zb{f}"
        if zb_pre is None:
            zb = M.sb(f"zb{f}", [128, HC * T], BF16, side="right")
            ln_stats_and_z(xt, xkey, zb, zkey, True, f"f{f}")
        else:
            zb = zb_pre
        xkeys = tuple(xkey(t) for t in range(TCN)) if callable(xkey) else (xkey,)
        P.dma("sync", s_xr.ap(), xt[:], reads=xkeys, writes=(f"s_xr{f}",))
        M.free(xt.name, P=P, keys=xkeys)
        if "z1" in taps and f == 0:
            P.dma("sync", d_taps["z1"], zb[:], reads=(zkey,), writes=("tapz1",))
        mxidx = 2 * f
        wgt = [M.sb(f"wgt{f}_{i}", [128, HC * 128], BF16) for i in range(3)]
        wut = [M.sb(f"wut{f}_{i}", [128, HC * 128], BF16) for i in range(3)]
        if f == 0:
            def _wq_env(ic, kind):
                st_, qi_, qb_ = wst[ic % 2], wqi[ic % 2], wqb[ic % 2]
                if kind == "g":
                    off = 0
                    rm_, rt_, rr_ = rmax[ic % 2], rtmp[ic % 2], rrec[ic % 2]
                    src_, ring, scr_, scol = d_wg, wgt, s_wgT, sgr
                    qin, tag, pt_ = "sync", "wg", ps_tr[0]
                else:
                    off = H
                    rm_, rt_, rr_ = umax[ic % 2], utmp[ic % 2], urec[ic % 2]
                    src_, ring, scr_, scol = d_wu, wut, s_wuT, sur
                    qin, tag, pt_ = "scalar", "wu", ps_tr[1]
                kst, kqi, kqb = (st_.name + kind), (qi_.name + kind), (qb_.name + kind)
                return (st_, qi_, qb_, off, rm_, rt_, rr_, src_, ring, scr_, scol,
                        qin, tag, pt_, kst, kqi, kqb)

            def fused_w1(ic, kind):
                (st_, qi_, qb_, off, rm_, rt_, rr_, src_, ring, scr_, scol,
                 qin, tag, pt_, kst, kqi, kqb) = _wq_env(ic, kind)
                osl = slice(off, off + H)
                P.dma(qin, st_[:, osl], src_[ic * 128:(ic + 1) * 128, :],
                      reads=(kst,), writes=(kst,))
                P.op("vector", lambda e, st_=st_, rm_=rm_, osl=osl: e.tensor_reduce(
                    out=rm_[:], in_=st_[:, osl], axis=AX.X, op=ALU.max,
                    apply_absolute_value=True), reads=(kst,), writes=(rm_.name,))
                P.op("vector", lambda e, rm_=rm_, rt_=rt_: e.reciprocal(
                    out=rt_[:], in_=rm_[:]), reads=(rm_.name,), writes=(rt_.name,))
                P.op("vector", lambda e, rt_=rt_, rr_=rr_: e.tensor_scalar(
                    out=rr_[:], in0=rt_[:], scalar1=QMAX, scalar2=None,
                    op0=ALU.mult), reads=(rt_.name,), writes=(rr_.name,))
                P.op("vector", lambda e, ic=ic, rm_=rm_, scol=scol: e.tensor_scalar(
                    out=col(scol, ic), in0=rm_[:], scalar1=1.0 / QMAX, scalar2=None,
                    op0=ALU.mult), reads=(rm_.name,), writes=(scol.name,))
                P.op("scalar", lambda e, st_=st_, qi_=qi_, rr_=rr_, osl=osl: e.activation(
                    out=qi_[:, osl], in_=st_[:, osl], func=AF.Copy, scale=rr_[:]),
                    reads=(kst, rr_.name), writes=(kqi,))
                P.op("vector", lambda e, qi_=qi_, qb_=qb_, osl=osl: e.tensor_scalar(
                    out=qb_[:, osl], in0=qi_[:, osl], scalar1=-128, scalar2=127,
                    op0=ALU.max, op1=ALU.min), reads=(kqi,), writes=(kqb,))

            def fused_w2(ic, kind):
                (st_, qi_, qb_, off, rm_, rt_, rr_, src_, ring, scr_, scol,
                 qin, tag, pt_, kst, kqi, kqb) = _wq_env(ic, kind)
                dst = ring[ic % 3]

                def trans(e, qb_=qb_, pt_=pt_, off=off):
                    for j in range(HC):
                        inst = e.transpose(pt_[:, j * 128:(j + 1) * 128],
                                           qb_[:, off + j * 128:off + (j + 1) * 128],
                                           eye[:])
                    return inst
                P.op("tensor", trans, reads=(kqb, "eye"), writes=(pt_.name,))
                P.op("vector", lambda e, pt_=pt_, dst=dst: e.tensor_copy(
                    out=dst[:], in_=pt_[:]), reads=(pt_.name,), writes=(dst.name,))
                P.dma("gpsimd", scr_.ap()[ic], dst[:],
                      reads=(dst.name,), writes=(f"scr_{tag}{ic}",))

            def fused_wtile(ic, kind):
                fused_w1(ic, kind)
                fused_w2(ic, kind)

            def head_start():
                for ic in (0, 1):
                    fused_wtile(ic, "g")
                    fused_wtile(ic, "u")
            global_max_scale(mxidx, 4 * HC, "zmx", sig_x, f"fx{f}", mid=head_start)
        else:
            global_max_scale(mxidx, 4 * HC, "zmx", sig_x, f"fx{f}")
        qof = 8 * f
        P.op("vector", lambda e: e.tensor_copy(out=col(qcols, qof + 0), in_=col(scc, 1)),
             reads=("scc1",), writes=(f"qc{f}0",))
        if f == 1:
            P.op("vector", lambda e: e.tensor_scalar(
                out=sgx[:, f * IC:(f + 1) * IC], in0=sgr[:], scalar1=col(scc, 0),
                scalar2=None, op0=ALU.mult), reads=("scc0", "sgr"), writes=(f"sgx{f}",))
            P.op("vector", lambda e: e.tensor_scalar(
                out=sux[:, f * IC:(f + 1) * IC], in0=sur[:], scalar1=col(scc, 0),
                scalar2=None, op0=ALU.mult), reads=("scc0", "sur"), writes=(f"sux{f}",))
        xqb = M.sb(f"xqb{f}", [128, HC * T], BF16)
        xqkey = f"xqb{f}"
        for tc in range(TCN):
            for hc in range(HC):
                pp = (tc * HC + hc) % 4
                i16_ = i16t[pp]
                sl = slice(hc * T + tc * TS, hc * T + (tc + 1) * TS)
                P.op("scalar", lambda e, sl=sl, i16_=i16_: e.activation(
                    out=i16_[:], in_=zb[:, sl], func=AF.Copy, scale=col(qcols, qof + 0)),
                    reads=(zkey, f"qc{f}0"), writes=(i16_.name,))
                P.op("vector", lambda e, sl=sl, i16_=i16_, tc=tc: e.tensor_scalar(
                    out=xqb[:, sl], in0=i16_[:], scalar1=-128, scalar2=127,
                    op0=ALU.max, op1=ALU.min),
                    reads=(i16_.name,), writes=(f"xq{f}t{tc}",))
        M.free(f"zb{f}", P=P, keys=(zkey,))
        if "xq1" in taps and f == 0:
            P.dma("sync", d_taps["xq1"], xqb[:],
                  reads=tuple(f"xq{f}t{tc}" for tc in range(TCN)), writes=("tapxq",))
        abw = [M.sb(f"abw{f}_{i}", [128, TS], BF16) for i in range(2)]
        psg = [M.ps(f"psg{f}_{i}", [128, TS], F32) for i in range(2)]
        psu = [M.ps(f"psu{f}_{i}", [128, TS], F32) for i in range(2)]
        sil = [M.sb(f"sil{f}_{i}", [128, TS], BF16) for i in range(2)]
        upt = [M.sb(f"upt{f}_{i}", [128, TS], BF16) for i in range(2)]
        for ic in range(IC):
            wg_b, wu_b = wgt[ic % 3], wut[ic % 3]
            if f == 0:
                if ic + 2 < IC:
                    fused_w1(ic + 2, "g")
                    fused_w1(ic + 2, "u")
                P.op("vector", lambda e, ic=ic: e.tensor_scalar(
                    out=col(sgx, ic), in0=col(sgr, ic), scalar1=col(scc, 0),
                    scalar2=None, op0=ALU.mult),
                    reads=("sgr", "scc0"), writes=(f"sgx{f}",))
                P.op("vector", lambda e, ic=ic: e.tensor_scalar(
                    out=col(sux, ic), in0=col(sur, ic), scalar1=col(scc, 0),
                    scalar2=None, op0=ALU.mult),
                    reads=("sur", "scc0"), writes=(f"sux{f}",))
            else:
                P.dma("gpsimd", wg_b[:], s_wgT.ap()[ic],
                      reads=(f"scr_wg{ic}",), writes=(wg_b.name,))
                P.dma("gpsimd", wu_b[:], s_wuT.ap()[ic],
                      reads=(f"scr_wu{ic}",), writes=(wu_b.name,))
            for tc in range(TCN):
                pp = (ic * TCN + tc) % 2
                pg_, pu_, sl_ = psg[pp], psu[pp], sil[pp]

                def mmg(e, wb=wg_b, tc=tc, pg_=pg_):
                    for hc in range(HC):
                        i = e.matmul(pg_[:], wb[:, hc * 128:(hc + 1) * 128],
                                     xqb[:, hc * T + tc * TS: hc * T + (tc + 1) * TS],
                                     start=(hc == 0), stop=(hc == HC - 1))
                    return i
                P.op("tensor", mmg, reads=(wg_b.name, f"xq{f}t{tc}"), writes=(pg_.name,))

                def mmu(e, wb=wu_b, tc=tc, pu_=pu_):
                    for hc in range(HC):
                        i = e.matmul(pu_[:], wb[:, hc * 128:(hc + 1) * 128],
                                     xqb[:, hc * T + tc * TS: hc * T + (tc + 1) * TS],
                                     start=(hc == 0), stop=(hc == HC - 1))
                    return i
                P.op("tensor", mmu, reads=(wu_b.name, f"xq{f}t{tc}"), writes=(pu_.name,))
                P.op("scalar", lambda e, ic=ic, pg_=pg_, sl_=sl_: e.activation(
                    out=sl_[:], in_=pg_[:], func=AF.Silu, scale=col(sgx, f * IC + ic)),
                    reads=(pg_.name, f"sgx{f}"), writes=(sl_.name,))
                up_ = upt[pp]
                P.op("scalar", lambda e, ic=ic, pu_=pu_, up_=up_: e.activation(
                    out=up_[:], in_=pu_[:], func=AF.Identity, scale=col(sux, f * IC + ic)),
                    reads=(pu_.name, f"sux{f}"), writes=(up_.name,))
                asl = slice(ic * T + tc * TS, ic * T + (tc + 1) * TS)
                aw_ = abw[pp]
                P.op("vector", lambda e, aw_=aw_, up_=up_, sl_=sl_: e.tensor_tensor(
                    out=aw_[:], in0=up_[:], in1=sl_[:], op=ALU.mult),
                    reads=(up_.name, sl_.name), writes=(aw_.name,))
                P.op("vector", lambda e, ic=ic, tc=tc, aw_=aw_: e.tensor_reduce(
                    out=col(amx, tc * IC + ic), in_=aw_[:],
                    axis=AX.X, op=ALU.max, apply_absolute_value=True),
                    reads=(aw_.name,), writes=("amx",))
                P.dma("scalar", s_ab.ap()[:, asl], aw_[:],
                      reads=(aw_.name,), writes=(f"sab{ic}_{tc}",))
            if f == 0 and ic + 2 < IC:
                fused_w2(ic + 2, "g")
                fused_w2(ic + 2, "u")
        _g = [f"xqb{f}", f"wgt{f}_0", f"wgt{f}_1", f"wgt{f}_2",
              f"wut{f}_0", f"wut{f}_1", f"wut{f}_2",
              f"psg{f}_0", f"psg{f}_1", f"psu{f}_0", f"psu{f}_1",
              f"sil{f}_0", f"sil{f}_1", f"upt{f}_0", f"upt{f}_1",
              f"abw{f}_0", f"abw{f}_1"]
        _gk = _g[1:] + [f"xq{f}t{tc}" for tc in range(TCN)]
        M.free(*_g, P=P, keys=tuple(_gk))
        if f == 0:
            run_w1d()
        P.op("vector", lambda e: e.tensor_reduce(
            out=zred[:], in_=amx[:, 0:TCN * IC], axis=AX.X, op=ALU.max),
            reads=("amx",), writes=("zred",))
        part_max_to_sct3("zred")
        idx = 2 * f + 1
        P.dma("sync", cin_mx[idx], sct[0:1, 3:4], reads=("sct3",), writes=(f"cinmx{idx}",))
        P.op("gpsimd", lambda e, idx=idx: e.collective_compute(
            "AllReduce", ALU.max, ins=[cin_mx[idx]], outs=[cout_mx[idx]],
            replica_groups=RG), reads=(f"cinmx{idx}",), writes=(f"coutmx{idx}",))
        P.dma("sync", gq[:], cout_mx[idx], reads=(f"coutmx{idx}",), writes=("gq",))
        P.op("vector", lambda e: e.tensor_tensor(
            out=sct[0:1, 0:1], in0=gq[:], in1=sig_h[:], op=ALU.mult),
            reads=("gq", "sig_h"), writes=("sct0",))
        P.op("vector", lambda e: e.tensor_scalar(
            out=sct[0:1, 1:2], in0=sct[0:1, 0:1], scalar1=1.0 / QMAX, scalar2=None,
            op0=ALU.mult), reads=("sct0",), writes=("sct1",))
        P.op("vector", lambda e: e.reciprocal(
            out=sct[0:1, 2:3], in_=sct[0:1, 1:2]),
            reads=("sct1",), writes=("sct2",))
        col_bcast(col(scc, 0), "scc0", sct[0:1, 1:2], "sct1", 0)
        col_bcast(col(scc, 1), "scc1", sct[0:1, 2:3], "sct2", 1)
        P.op("vector", lambda e: e.tensor_copy(out=col(qcols, qof + 3), in_=col(scc, 1)),
             reads=("scc1",), writes=(f"qc{f}3",))
        P.op("vector", lambda e: e.tensor_scalar(
            out=sdh[:, f * HC:(f + 1) * HC], in0=sdr[:], scalar1=col(scc, 0),
            scalar2=0.5, op0=ALU.mult, op1=ALU.mult),
            reads=("scc0", "sdr"), writes=(f"sdh{f}",))
        obuf = [M.sb(f"obuf{f}_{i}", [128, TS], F32) for i in range(2)] if final else None
        abr = [M.sb(f"abr{f}_{i}", [128, TS], BF16) for i in range(4)]
        hqb = M.sb(f"hqb{f}", [128, IC * TS], BF16)
        wdt = [M.sb(f"wdt{f}_{i}", [128, IC * 128], BF16) for i in range(2)]
        psd = [M.ps(f"psd{f}_{i}", [128, TS], F32) for i in range(2)]
        xrr = [M.sb(f"xrr{f}_{i}", [128, TS], BF16) for i in range(2)]
        for tc in range(TCN):
            for ic in range(IC):
                pp = (tc * IC + ic) % 4
                i16_ = i16t[pp]
                ar_ = abr[pp]
                asl = slice(ic * T + tc * TS, ic * T + (tc + 1) * TS)
                P.dma(("sync", "gpsimd")[ic % 2], ar_[:], s_ab.ap()[:, asl],
                      reads=(f"sab{ic}_{tc}",), writes=(ar_.name,))
                P.op("scalar", lambda e, ar_=ar_, i16_=i16_: e.activation(
                    out=i16_[:], in_=ar_[:], func=AF.Copy, scale=col(qcols, qof + 3)),
                    reads=(ar_.name, f"qc{f}3"), writes=(i16_.name,))
                P.op("vector", lambda e, ic=ic, i16_=i16_: e.tensor_scalar(
                    out=hqb[:, ic * TS:(ic + 1) * TS], in0=i16_[:],
                    scalar1=-128, scalar2=127, op0=ALU.max, op1=ALU.min),
                    reads=(i16_.name,), writes=(f"hqb{f}c{ic // 6}",))
            for hc in range(HC):
                wd_b = wdt[hc % 2]
                P.dma("gpsimd", wd_b[:], s_wdT.ap()[hc],
                      reads=(f"scr_wd{hc}",), writes=(wd_b.name,))
                xr_ = xrr[hc % 2]
                P.dma("sync", xr_[:],
                      s_xr.ap()[:, hc * T + tc * TS: hc * T + (tc + 1) * TS],
                      reads=(f"s_xr{f}",), writes=(xr_.name,))

                pd_ = psd[hc % 2]
                for g4 in range(4):
                    ic_lo, ic_hi = g4 * 6, min((g4 + 1) * 6, IC)

                    def mmd(e, wb=wd_b, pd_=pd_, ic_lo=ic_lo, ic_hi=ic_hi):
                        for ic in range(ic_lo, ic_hi):
                            i = e.matmul(pd_[:], wb[:, ic * 128:(ic + 1) * 128],
                                         hqb[:, ic * TS:(ic + 1) * TS],
                                         start=(ic == 0), stop=(ic == IC - 1))
                        return i
                    P.op("tensor", mmd, reads=(wd_b.name, f"hqb{f}c{g4}"),
                         writes=(pd_.name,))
                xsl = slice(hc * T + tc * TS, hc * T + (tc + 1) * TS)
                if final:
                    ob_ = obuf[hc % 2]
                    P.op("vector", lambda e, hc=hc, xr_=xr_, pd_=pd_, ob_=ob_: e.scalar_tensor_tensor(
                        out=ob_[:], in0=pd_[:], scalar=col(sdh, f * HC + hc),
                        in1=xr_[:], op0=ALU.mult, op1=ALU.add),
                        reads=(pd_.name, f"sdh{f}", xr_.name), writes=(ob_.name,))
                    P.dma("sync",
                          d_y[hc * 128:(hc + 1) * 128, tc * TS:(tc + 1) * TS],
                          ob_[:], reads=(ob_.name,), writes=("y_out",))
                else:
                    P.op("vector", lambda e, hc=hc, xsl=xsl, xr_=xr_, pd_=pd_: e.scalar_tensor_tensor(
                        out=x_new[:, xsl], in0=pd_[:], scalar=col(sdh, f * HC + hc),
                        in1=xr_[:], op0=ALU.mult, op1=ALU.add),
                        reads=(pd_.name, f"sdh{f}", xr_.name), writes=(x_new_key,))
        fl = [f"abr{f}_0", f"abr{f}_1", f"abr{f}_2", f"abr{f}_3",
              f"hqb{f}", f"wdt{f}_0", f"wdt{f}_1",
              f"psd{f}_0", f"psd{f}_1",
              f"xrr{f}_0", f"xrr{f}_1"] + ([f"obuf{f}_0", f"obuf{f}_1"] if final else [])
        M.free(*fl, P=P, keys=tuple(fl))

    wst = [M.sb(f"wst{i}", [128, I_FFN], F32) for i in range(2)]
    wqi = [M.sb(f"wqi{i}", [128, I_FFN], I16) for i in range(2)]
    wqb = [M.sb(f"wqb{i}", [128, I_FFN], BF16) for i in range(2)]
    wtr = [M.sb(f"wtr{i}", [128, 1024], BF16) for i in range(2)]
    rmax = [M.sb(f"rmax{i}", [128, 1], F32) for i in range(2)]
    rtmp = [M.sb(f"rtmp{i}", [128, 1], F32) for i in range(2)]
    rrec = [M.sb(f"rrec{i}", [128, 1], F32) for i in range(2)]
    umax = [M.sb(f"umax{i}", [128, 1], F32) for i in range(2)]
    utmp = [M.sb(f"utmp{i}", [128, 1], F32) for i in range(2)]
    urec = [M.sb(f"urec{i}", [128, 1], F32) for i in range(2)]
    ps_tr = [M.ps(f"ps_tr{i}", [128, 1024], BF16) for i in range(2)]

    x0 = M.sb("x0", [128, HC * T], BF16)
    xf32 = [M.sb(f"xf32_{i}", [128, TS], F32) for i in range(4)]
    for tc in range(TCN):
        for hc in range(HC):
            xf = xf32[(tc * HC + hc) % 4]
            P.dma(("sync", "scalar")[hc % 2], xf[:],
                  d_x[hc * 128:(hc + 1) * 128, tc * TS:(tc + 1) * TS],
                  reads=(xf.name,), writes=(xf.name,))
            P.op("vector", lambda e, hc=hc, tc=tc, xf=xf: e.tensor_copy(
                out=x0[:, hc * T + tc * TS: hc * T + (tc + 1) * TS], in_=xf[:]),
                reads=(xf.name,), writes=(f"x0t{tc}",))
    M.free("xf32_0", "xf32_1", "xf32_2", "xf32_3", P=P,
           keys=("xf32_0", "xf32_1", "xf32_2", "xf32_3"))

    x1 = M.sb("x1", [128, HC * T], BF16, side="right")
    ffn(0, x0, lambda tc: f"x0t{tc}", x1, "x1", final=False)
    if "x1" in taps:
        P.dma("sync", d_taps["x1"], x1[:], reads=("x1",), writes=("tapx1",))

    zb2 = M.sb("zb2", [128, HC * T], BF16)
    ln_stats_and_z(x1, "x1", zb2, "zb2", False, "m0")
    P.dma("sync", s_x1.ap(), x1[:], reads=("x1",), writes=("s_x1",))
    M.free("x1", P=P, keys=("x1",))

    xin = M.sb("xin", [128, DC * T], BF16, side="right")
    wif = [M.sb(f"wif{i}", [128, HC * 128], F32) for i in range(2)]
    wib = [M.sb(f"wib{i}", [128, HC * 128], BF16) for i in range(2)]
    psi = [M.ps(f"psi{i}", [128, TS], F32) for i in range(2)]
    srt = [M.sb(f"srt{i}", [128, T], BF16) for i in range(2)]
    for oc in range(2 * DC):
        wb, wf = wib[oc % 2], wif[oc % 2]
        P.dma("gpsimd", wf[:], d_win[oc], reads=(wf.name,), writes=(wf.name,))
        P.op("vector", lambda e, wb=wb, wf=wf: e.tensor_copy(out=wb[:], in_=wf[:]),
             reads=(wf.name,), writes=(wb.name,))
        for tc in range(TCN):
            pi_ = psi[tc % 2]

            def mmi(e, wb=wb, tc=tc, pi_=pi_):
                for hc in range(HC):
                    i = e.matmul(pi_[:], wb[:, hc * 128:(hc + 1) * 128],
                                 zb2[:, hc * T + tc * TS: hc * T + (tc + 1) * TS],
                                 start=(hc == 0), stop=(hc == HC - 1))
                return i
            P.op("tensor", mmi, reads=(wb.name, "zb2"), writes=(pi_.name,))
            if oc < DC:
                P.op("scalar", lambda e, oc=oc, tc=tc, pi_=pi_: e.activation(
                    out=xin[:, oc * T + tc * TS: oc * T + (tc + 1) * TS],
                    in_=pi_[:], func=AF.Identity, bias=col(c_bin, oc)),
                    reads=(pi_.name, "c_bin"), writes=("xin",))
            else:
                sb_ = srt[oc % 2]
                P.op("scalar", lambda e, oc=oc, tc=tc, sb_=sb_, pi_=pi_: e.activation(
                    out=sb_[:, tc * TS:(tc + 1) * TS], in_=pi_[:], func=AF.Silu,
                    bias=col(c_bin, oc)),
                    reads=(pi_.name, "c_bin"), writes=(sb_.name,))
        if oc >= DC:
            sb_ = srt[oc % 2]
            P.dma("sync", s_sres.ap()[:, (oc - DC) * T:(oc - DC + 1) * T], sb_[:],
                  reads=(sb_.name,), writes=("s_sres",))
        if oc == DC - 1:
            bndst = M.sb("bndst", [128, DC * 3], BF16, side="right")
            P.op("vector", lambda e: e.tensor_copy(
                out=bndst.ap().rearrange("p (d c) -> p d c", d=DC),
                in_=xin.ap().rearrange("p (d t) -> p d t", d=DC)[:, :, T - 3:T]),
                reads=("xin",), writes=("bndst",))
            P.dma("sync", cin_bnd, bndst[:], reads=("bndst",), writes=("cin_bnd",))
            P.op("gpsimd", lambda e: e.collective_compute(
                "AllGather", ALU.bypass, ins=[cin_bnd], outs=[cout_bnd],
                replica_groups=RG), reads=("cin_bnd",), writes=("cout_bnd",))
    M.free("wif0", "wif1", "wib0", "wib1", "srt0", "srt1", "psi0", "psi1", P=P,
           keys=("wif0", "wif1", "wib0", "wib1", "srt0", "srt1", "psi0", "psi1"))
    M.free("zb2", P=P, keys=("zb2",))
    if "xin" in taps:
        P.dma("sync", d_taps["xin"], xin[:], reads=("xin",), writes=("tapxin",))

    bndr = M.sb("bndr", [128, DC * 3], BF16, side="right")
    bndm = M.sb("bndm", [128, DC * 3], BF16, side="right")

    def dyn_bnd(e):
        return e.dma_start(
            out=bndr[:],
            in_=bass.AP(cout_bnd.tensor, regs["bnd"], [[DC * 3, 128], [1, DC * 3]]))
    P.dyn_dma("gpsimd", dyn_bnd, reads=("cout_bnd",), writes=("bndr",))
    P.op("vector", lambda e: e.tensor_scalar(
        out=bndm[:], in0=bndr[:], scalar1=col(cmask, 0), scalar2=None, op0=ALU.mult),
        reads=("bndr",), writes=("bndm",))

    xe = M.sb("xe", [128, T + 3], BF16)
    cv1 = M.sb("cv1", [128, T], BF16)
    cv2 = M.sb("cv2", [128, T], BF16)
    cv3 = M.sb("cv3", [128, T], BF16)
    for dc in range(DC):
        P.op("vector", lambda e, dc=dc: e.tensor_copy(
            out=xe[:, 0:3], in_=bndm[:, dc * 3:(dc + 1) * 3]),
            reads=("bndm",), writes=("xe",))
        P.op("vector", lambda e, dc=dc: e.tensor_copy(
            out=xe[:, 3:T + 3], in_=xin[:, dc * T:(dc + 1) * T]),
            reads=("xin",), writes=("xe",))
        P.op("vector", lambda e, dc=dc: e.tensor_scalar(
            out=cv1[:], in0=xe[:, 0:T], scalar1=col(c_cw, dc * 4 + 0), scalar2=None,
            op0=ALU.mult), reads=("xe", "c_cw"), writes=("cv1",))
        for k in (1, 2, 3):
            s_, d_ = (cv1, cv2) if k % 2 == 1 else (cv2, cv1)
            P.op("vector", lambda e, dc=dc, k=k: e.tensor_scalar(
                out=cv3[:], in0=xe[:, k:T + k], scalar1=col(c_cw, dc * 4 + k),
                scalar2=None, op0=ALU.mult), reads=("xe", "c_cw"), writes=("cv3",))
            P.op("vector", lambda e, s_=s_, d_=d_: e.tensor_tensor(
                out=d_[:], in0=cv3[:], in1=s_[:], op=ALU.add),
                reads=("cv3", s_.name), writes=(d_.name,))
        P.op("scalar", lambda e, dc=dc: e.activation(
            out=xin[:, dc * T:(dc + 1) * T], in_=cv2[:], func=AF.Silu,
            bias=col(c_cb, dc)), reads=("cv2", "c_cb"), writes=("xin",))
    M.free("xe", "cv1", "cv2", "cv3", "bndr", "bndm", "bndst", P=P,
           keys=("xe", "cv1", "cv2", "cv3", "bndr", "bndm", "bndst"))
    if "xc" in taps:
        P.dma("sync", d_taps["xc"], xin[:], reads=("xin",), writes=("tapxc",))

    xpw_b = M.sb("xpw_b", [128, DC * 48], BF16)
    xpf = M.sb("xpf", [128, 48], F32, side="right")
    for dc in range(DC):
        P.dma("sync", xpf[:], d_xpw[dc * 128:(dc + 1) * 128, :], writes=("xpf",))
        P.op("vector", lambda e, dc=dc: e.tensor_copy(
            out=xpw_b[:, dc * 48:(dc + 1) * 48], in_=xpf[:]),
            reads=("xpf",), writes=("xpw_b",))
    dtw_f = M.sb("dtw_f", [NST, D_IN], F32, side="right")
    dtw_b = M.sb("dtw_b", [NST, D_IN], BF16)
    P.dma("sync", dtw_f[:], d_dtw, writes=("dtw_f",))
    P.op("vector", lambda e: e.tensor_copy(out=dtw_b[:], in_=dtw_f[:]),
         reads=("dtw_f",), writes=("dtw_b",))
    M.free("xpf", "dtw_f", P=P, keys=("xpf", "dtw_f"))

    prowf = M.sb("prowf", [NST, T], F32, side="right")
    draw_b = M.sb("draw_b", [NST, T], BF16)
    brow_b = M.sb("brow_b", [NST, T], BF16)
    crow_b = M.sb("crow_b", [NST, T], BF16)
    psp = M.ps("psp", [NST, TS], F32)
    for g, dst in ((0, draw_b), (1, brow_b), (2, crow_b)):
        for tc in range(TCN):
            def mmp(e, g=g, tc=tc):
                for dc in range(DC):
                    i = e.matmul(psp[:], xpw_b[:, dc * 48 + g * NST: dc * 48 + (g + 1) * NST],
                                 xin[:, dc * T + tc * TS: dc * T + (tc + 1) * TS],
                                 start=(dc == 0), stop=(dc == DC - 1))
                return i
            P.op("tensor", mmp, reads=("xpw_b", "xin"), writes=("psp",))
            P.op("scalar", lambda e, g=g, tc=tc: e.activation(
                out=prowf[:, tc * TS:(tc + 1) * TS], in_=psp[:], func=AF.Identity,
                bias=c_xpb3[0:NST, g:g + 1]), reads=("psp", "c_xpb3"), writes=("prowf",))
        P.op("vector", lambda e, dst=dst: e.tensor_copy(out=dst[:], in_=prowf[:]),
             reads=("prowf",), writes=(dst.name,))
    M.free("prowf", P=P, keys=("prowf",))
    P.dma("sync", s_bc.ap()[0], brow_b[:], reads=("brow_b",), writes=("s_bc",))
    P.dma("sync", s_bc.ap()[1], crow_b[:], reads=("crow_b",), writes=("s_bc",))
    M.free("brow_b", "crow_b", P=P, keys=("brow_b", "crow_b"))

    pbct = [M.sb(f"pbct{i}", [128, TS], BF16) for i in range(4)]
    rb1 = [M.sb(f"rb1_{i}", [1, T], BF16) for i in range(4)]
    ps_bb = [M.ps(f"ps_bb{i}", [128, TS], F32) for i in range(4)]
    rows = [(bc, n) for bc in range(2) for n in range(NST)]

    def row_dma(i):
        bc, n = rows[i]
        r_ = rb1[i % 4]
        P.dma("sync", r_[:], s_bc.ap()[bc, n], reads=(r_.name, "s_bc"),
              writes=(r_.name,))
    for i in range(3):
        row_dma(i)
    for i, (bc, n) in enumerate(rows):
        scr = s_Bb if bc == 0 else s_Cb
        skey = "s_Bb" if bc == 0 else "s_Cb"
        r_ = rb1[i % 4]
        if i + 3 < len(rows):
            row_dma(i + 3)
        for q in range(TCN):
            pb_, pq_ = pbct[q], ps_bb[q]
            P.op("tensor", lambda e, r_=r_, q=q, pq_=pq_: e.matmul(
                pq_[:], ones1[:], r_[0:1, q * TS:(q + 1) * TS],
                start=True, stop=True),
                reads=(r_.name, "ones1"), writes=(pq_.name,))
            if q % 2 == 0:
                P.op("scalar", lambda e, pb_=pb_, pq_=pq_: e.activation(
                    out=pb_[:], in_=pq_[:], func=AF.Identity),
                    reads=(pq_.name,), writes=(pb_.name,))
            else:
                P.op("vector", lambda e, pb_=pb_, pq_=pq_: e.tensor_copy(
                    out=pb_[:], in_=pq_[:]), reads=(pq_.name,), writes=(pb_.name,))
            P.dma(("scalar", "gpsimd")[q % 2],
                  scr.ap()[n, q // 2][:, (q % 2) * TS:(q % 2 + 1) * TS],
                  pb_[:], reads=(pb_.name,), writes=(skey,))
    M.free("pbct0", "pbct1", "pbct2", "pbct3", "rb1_0", "rb1_1", "rb1_2", "rb1_3",
           "ps_bb0", "ps_bb1", "ps_bb2", "ps_bb3",
           P=P, keys=("pbct0", "pbct1", "pbct2", "pbct3",
                      "rb1_0", "rb1_1", "rb1_2", "rb1_3",
                      "ps_bb0", "ps_bb1", "ps_bb2", "ps_bb3"))
    if "bcB" in taps:
        bbt = M.sb("bbt", [128, T], BF16)
        for q in range(2):
            P.dma("sync", bbt[:, q * 1024:(q + 1) * 1024], s_Bb.ap()[0, q],
                  reads=("s_Bb",), writes=("bbt",))
        P.dma("sync", d_taps["bcB"], bbt[:], reads=("bbt",), writes=("tapbcB",))
        M.free("bbt", P=P, keys=("bbt",))

    TH = 1024
    NCH = T // TH
    for dc in range(DC):
        P.dma("sync", s_xc.ap()[:, dc * T:(dc + 1) * T], xin[:, dc * T:(dc + 1) * T],
              reads=("xin",), writes=("s_xc",))
    M.free("xin", P=P, keys=("xin",))

    zeros_t = M.sb("zeros_t", [128, TH], BF16)
    P.op("vector", lambda e: e.memset(zeros_t[:], 0.0), writes=("zeros_t",))
    delta2 = [M.sb(f"delta{i}", [128, T], F32) for i in range(2)]
    ets = [M.sb("et0", [128, TS], F32)] * 2
    sloc2 = [M.sb(f"sloc{i}", [128, T], F32) for i in range(2)]
    u2 = [M.sb(f"u{i}", [128, TH], BF16) for i in range(2)]
    hbuf = M.sb("hbuf", [128, NST * T], BF16)
    da = [M.sb(f"da{i}", [128, TH], BF16) for i in range(3)]
    dec = [M.sb(f"dec{i}", [128, TH], BF16) for i in range(2)]
    dbx = [M.sb(f"dbx{i}", [128, TH], BF16) for i in range(4)]
    bbr = [M.sb(f"bbr{i}", [128, TH], BF16) for i in range(3)]
    cbr = [M.sb(f"cbr{i}", [128, TH], BF16) for i in range(3)]
    xcr = [M.sb(f"xcr{i}", [128, TH], BF16) for i in range(4)]
    gat = [M.sb(f"gat{i}", [128, TH], BF16) for i in range(2)]
    srr = [M.sb(f"srr{i}", [128, TH], BF16) for i in range(2)]
    ya = M.sb("ya", [128, TH], BF16)
    stg2 = [M.sb(f"stg{i}", [128, 2 * NST], F32) for i in range(2)]
    ptmp = M.sb("ptmp", [128, NST], F32)
    g3s = [M.sb(f"g3_{j}", [128, 2 * NST], F32) for j in range(3)]
    gms = M.sb("gms", [128, 3 * 2 * NST], F32)
    ct1 = M.sb("ct1", [128, NST], F32)
    ct2 = M.sb("ct2", [128, NST], F32)
    hin2 = [M.sb(f"hin{i}", [128, NST], F32) for i in range(2)]
    hft = [M.sb(f"hft{i}", [128, TH], BF16) for i in range(2)]
    hfr = [M.sb(f"hfr{i}", [128, TH], BF16) for i in range(2)]
    ymr = [M.sb(f"ymr{i}", [128, TH], BF16) for i in range(2)]
    ybf = M.sb("ybf", [128, TH], BF16)
    yps = M.ps("yps", [128, TH], F32)
    dtps = [M.ps(f"dtps{i}", [128, TS], F32) for i in range(2)]

    def m4_pre(dc):
        pp = dc % 2
        delta, sloc = delta2[pp], sloc2[pp]
        for tc in range(TCN):
            dp_, et_ = dtps[tc % 2], ets[tc % 2]
            P.op("tensor", lambda e, dc=dc, tc=tc, dp_=dp_: e.matmul(
                dp_[:], dtw_b[:, dc * 128:(dc + 1) * 128],
                draw_b[0:NST, tc * TS:(tc + 1) * TS], start=True, stop=True),
                reads=("dtw_b", "draw_b"), writes=(dp_.name,))
            P.op("scalar", lambda e, dc=dc, dp_=dp_, et_=et_: e.activation(
                out=et_[:], in_=dp_[:], func=AF.Exp, bias=col(c_dtb, dc)),
                reads=(dp_.name, "c_dtb"), writes=(et_.name,))
            P.op("scalar", lambda e, tc=tc, et_=et_, delta=delta: e.activation(
                out=delta[:, tc * TS:(tc + 1) * TS], in_=et_[:], func=AF.Ln,
                bias=1.0), reads=(et_.name,), writes=(delta.name,))
        for ch in range(NCH):
            tsl = slice(ch * TH, (ch + 1) * TH)
            P.op("vector", lambda e, ch=ch, tsl=tsl, delta=delta, sloc=sloc:
                 e.tensor_tensor_scan(
                     out=sloc[:, tsl], data0=zeros_t[:], data1=delta[:, tsl],
                     initial=(0.0 if ch == 0 else sloc[:, ch * TH - 1: ch * TH]),
                     op0=ALU.add, op1=ALU.add),
                 reads=("zeros_t", delta.name) + ((sloc.name,) if ch else ()),
                 writes=(sloc.name,))
        if "delta0" in taps and dc == 0:
            P.dma("sync", d_taps["delta0"], delta[:], reads=(delta.name,), writes=("tapd0",))
        if "sloc0" in taps and dc == 0:
            P.dma("sync", d_taps["sloc0"], sloc[:], reads=(sloc.name,), writes=("tapsl",))

    def m4_merged(dcA, dcB):
        if dcA is not None:
            ppA = dcA % 2
            deltaA, slocA, stgA = delta2[ppA], sloc2[ppA], stg2[ppA]
        if dcB is not None:
            ppB = dcB % 2
            slocB, hinB = sloc2[ppB], hin2[ppB]
        for ch in range(NCH):
            tsl = slice(ch * TH, (ch + 1) * TH)
            if dcA is not None:
                xcA = xcr[(dcA % 2) * 2 + ch]
                P.dma("sync", xcA[:],
                      s_xc.ap()[:, dcA * T + ch * TH: dcA * T + (ch + 1) * TH],
                      reads=("s_xc",), writes=(xcA.name,))
                ub_ = u2[ch % 2]
                P.op("vector", lambda e, tsl=tsl, deltaA=deltaA, ub_=ub_, xcA=xcA:
                     e.tensor_tensor(
                         out=ub_[:], in0=deltaA[:, tsl], in1=xcA[:], op=ALU.mult),
                     reads=(deltaA.name, xcA.name), writes=(ub_.name,))
            if dcB is not None:
                sr_ = srr[ch % 2]
                P.dma("sync", sr_[:],
                      s_sres.ap()[:, dcB * T + ch * TH: dcB * T + (ch + 1) * TH],
                      reads=("s_sres",), writes=(sr_.name,))
                xcB = xcr[(dcB % 2) * 2 + ch]
            for n in range(NST):
                if dcB is not None:
                    cb_, de_, hf_ = cbr[n % 3], dec[n % 2], hfr[n % 2]
                    P.dma("scalar", cb_[:], s_Cb.ap()[n, ch], reads=("s_Cb",),
                          writes=(cb_.name,))
                    P.op("scalar", lambda e, dcB=dcB, n=n, de_=de_, tsl=tsl, slocB=slocB:
                         e.activation(
                             out=de_[:], in_=slocB[:, tsl], func=AF.Exp,
                             scale=col(acols, dcB * NST + n)),
                         reads=(slocB.name, "acols"), writes=(de_.name,))
                    hsl = slice(n * T + ch * TH, n * T + (ch + 1) * TH)
                    t1_ = hft[n % 2]
                    P.op("scalar", lambda e, n=n, de_=de_, t1_=t1_, hinB=hinB:
                         e.activation(
                             out=t1_[:], in_=de_[:], func=AF.Identity,
                             scale=col(hinB, n)),
                         reads=(de_.name, hinB.name), writes=(t1_.name,))
                    P.op("vector", lambda e, hf_=hf_, t1_=t1_, hsl=hsl, n=n:
                         e.tensor_tensor(
                             out=hf_[:], in0=t1_[:], in1=hbuf[:, hsl], op=ALU.add),
                         reads=(t1_.name, f"hb{n}"), writes=(hf_.name,))
                    ym_ = ymr[n % 2]
                    P.op("vector", lambda e, hf_=hf_, cb_=cb_, ym_=ym_: e.tensor_tensor(
                        out=ym_[:], in0=hf_[:], in1=cb_[:], op=ALU.mult),
                        reads=(hf_.name, cb_.name), writes=(ym_.name,))

                    def mmacc(e, ym_=ym_, n=n):
                        i = e.matmul(yps[:, 0:TS], eye[:], ym_[:, 0:TS],
                                     start=(n == 0), stop=(n == NST - 1))
                        i = e.matmul(yps[:, TS:TH], eye[:], ym_[:, TS:TH],
                                     start=(n == 0), stop=(n == NST - 1))
                        return i
                    P.op("tensor", mmacc, reads=(ym_.name, "eye"), writes=("yps",))
                if dcA is not None:
                    bb_, da_, dx_ = bbr[n % 3], da[n % 3], dbx[n % 4]
                    P.dma("sync", bb_[:], s_Bb.ap()[n, ch], reads=("s_Bb",),
                          writes=(bb_.name,))
                    P.op("scalar", lambda e, dcA=dcA, n=n, da_=da_, deltaA=deltaA,
                         tsl=tsl: e.activation(
                             out=da_[:], in_=deltaA[:, tsl], func=AF.Exp,
                             scale=col(acols, dcA * NST + n)),
                         reads=(deltaA.name, "acols"), writes=(da_.name,))
                    ub_ = u2[ch % 2]
                    P.op("vector", lambda e, dx_=dx_, bb_=bb_, ub_=ub_:
                         e.tensor_tensor(
                             out=dx_[:], in0=ub_[:], in1=bb_[:], op=ALU.mult),
                         reads=(ub_.name, bb_.name), writes=(dx_.name,))
                    hsl = slice(n * T + ch * TH, n * T + (ch + 1) * TH)
                    P.op("vector", lambda e, hsl=hsl, ch=ch, n=n, da_=da_, dx_=dx_:
                         e.tensor_tensor_scan(
                             out=hbuf[:, hsl], data0=da_[:], data1=dx_[:],
                             initial=(0.0 if ch == 0 else
                                      hbuf[:, n * T + ch * TH - 1: n * T + ch * TH]),
                             op0=ALU.mult, op1=ALU.add),
                         reads=(da_.name, dx_.name, f"hb{n}"), writes=(f"hb{n}",))
                    if ch == NCH - 1:
                        P.op("vector", lambda e, n=n, stgA=stgA: e.tensor_copy(
                            out=col(stgA, n), in_=hbuf[:, n * T + T - 1: n * T + T]),
                            reads=(f"hb{n}",), writes=(stgA.name,))
            if dcB is not None:
                ssl = slice(ch * TH, (ch + 1) * TH)
                P.op("scalar", lambda e: e.activation(
                    out=ybf[:], in_=yps[:], func=AF.Identity),
                    reads=("yps",), writes=("ybf",))
                tds_ = dec[0]
                P.op("vector", lambda e, dcB=dcB, xcB=xcB, tds_=tds_: e.tensor_scalar(
                    out=tds_[:], in0=xcB[:], scalar1=col(c_dsk, dcB), scalar2=None,
                    op0=ALU.mult), reads=(xcB.name, "c_dsk"), writes=(tds_.name,))
                P.op("vector", lambda e, tds_=tds_: e.tensor_tensor(
                    out=ya[:], in0=tds_[:], in1=ybf[:], op=ALU.add),
                    reads=(tds_.name, "ybf"), writes=("ya",))
                if "y0" in taps and dcB == 0:
                    P.dma("sync", d_taps["y0"][:, ssl], ya[:],
                          reads=("ya",), writes=("tapy0",))
                g_ = gat[ch % 2]
                P.op("vector", lambda e, sr_=sr_, g_=g_: e.tensor_tensor(
                    out=g_[:], in0=ya[:], in1=sr_[:], op=ALU.mult),
                    reads=("ya", sr_.name), writes=(g_.name,))
                P.dma("sync", s_xc.ap()[:, dcB * T + ch * TH: dcB * T + (ch + 1) * TH],
                      g_[:], reads=(g_.name,), writes=("s_xc",))
        if dcA is not None and "h0" in taps and dcA == 0:
            P.dma("sync", d_taps["h0"], hbuf[:, 0:T], reads=("hb0",), writes=("taph0",))

    def m4_exchange(dc):
        pp = dc % 2
        sloc, stg, hin = sloc2[pp], stg2[pp], hin2[pp]
        P.op("vector", lambda e, dc=dc, sloc=sloc: e.tensor_scalar(
            out=ptmp[:], in0=acols[:, dc * NST:(dc + 1) * NST],
            scalar1=sloc[:, T - 1:T], scalar2=None, op0=ALU.mult),
            reads=("acols", sloc.name), writes=("ptmp",))
        P.op("scalar", lambda e, stg=stg: e.activation(
            out=stg[:, NST:2 * NST], in_=ptmp[:], func=AF.Exp),
            reads=("ptmp",), writes=(stg.name,))
        P.dma("sync", cin_st[dc], stg[:], reads=(stg.name,), writes=(f"cinst{dc}",))
        P.op("gpsimd", lambda e, dc=dc: e.collective_compute(
            "AllGather", ALU.bypass, ins=[cin_st[dc]], outs=[cout_st[dc].ap()],
            replica_groups=RG), reads=(f"cinst{dc}",), writes=(f"coutst{dc}",))

    def m4_combine(dc):
        hin = hin2[dc % 2]
        dyn_eng = ("gpsimd", "scalar", "sync")[dc % 3]
        for k in (1, 2, 3):
            j = 3 - k

            def dyn_car(e, dc=dc, k=k, j=j, dyn_eng=dyn_eng):
                return e.dma_start(
                    out=g3s[j][:],
                    in_=bass.AP(cout_st[dc].ap().tensor, regs[(dyn_eng, k)],
                                [[2 * NST, 128], [1, 2 * NST]]))
            P.dyn_dma(dyn_eng, dyn_car, reads=(f"coutst{dc}", f"g3_{j}"),
                      writes=(f"g3_{j}",))
        for j in range(3):
            P.op("vector", lambda e, j=j: e.tensor_scalar(
                out=gms[:, j * 32:(j + 1) * 32], in0=g3s[j][:],
                scalar1=col(cmask, 2 - j), scalar2=None, op0=ALU.mult),
                reads=(f"g3_{j}", "cmask"), writes=("gms",))
        P.op("vector", lambda e: e.tensor_tensor(
            out=ct1[:], in0=gms[:, 48:64], in1=gms[:, 0:16], op=ALU.mult),
            reads=("gms",), writes=("ct1",))
        P.op("vector", lambda e: e.tensor_tensor(
            out=ct2[:], in0=gms[:, 32:48], in1=ct1[:], op=ALU.add),
            reads=("gms", "ct1"), writes=("ct2",))
        P.op("vector", lambda e: e.tensor_tensor(
            out=ct1[:], in0=gms[:, 80:96], in1=ct2[:], op=ALU.mult),
            reads=("gms", "ct2"), writes=("ct1",))
        P.op("vector", lambda e, hin=hin: e.tensor_tensor(
            out=hin[:], in0=gms[:, 64:80], in1=ct1[:], op=ALU.add),
            reads=("gms", "ct1"), writes=(hin.name,))

    for dc in range(DC):
        m4_pre(dc)
        if dc > 0:
            m4_combine(dc - 1)
        m4_merged(dc, dc - 1 if dc > 0 else None)
        m4_exchange(dc)
    m4_combine(DC - 1)
    m4_merged(None, DC - 1)

    _m4f = ["zeros_t", "delta0", "delta1", "et0", "sloc0", "sloc1",
            "u0", "u1", "hbuf", "da0", "da1", "da2", "dec0", "dec1",
            "dbx0", "dbx1", "dbx2", "dbx3", "bbr0", "bbr1", "bbr2", "cbr0", "cbr1", "cbr2",
            "xcr0", "xcr1", "xcr2", "xcr3", "gat0", "gat1", "srr0", "srr1", "ya", "ybf", "yps", "hfr0", "hfr1", "ymr0", "ymr1",
            "stg0", "stg1", "ptmp", "g3_0", "g3_1", "g3_2", "gms", "ct1",
            "ct2", "hin0", "hin1", "hft0", "hft1", "dtps0", "dtps1", "draw_b", "xpw_b", "dtw_b", "psp"]
    _m4k = _m4f + [f"hb{n}" for n in range(NST)]
    M.free(*_m4f, P=P, keys=tuple(_m4k))
    if "gated" in taps:
        gt = M.sb("gt", [128, T], BF16)
        for dc in range(DC):
            P.dma("sync", gt[:], s_xc.ap()[:, dc * T:(dc + 1) * T],
                  reads=("s_xc",), writes=("gt",))
            P.dma("sync", d_taps["gated"][:, dc * T:(dc + 1) * T], gt[:],
                  reads=("gt",), writes=("tapg",))
        M.free("gt", P=P, keys=("gt",))

    x2 = M.sb("x2", [128, HC * T], BF16)
    zb1 = M.sb("zb1", [128, HC * T], BF16, side="right")
    woall = M.sb("woall", [128, HC * DC * 128], BF16)
    wof = [M.sb(f"wof{i}", [128, DC * 128], F32, side="right") for i in range(2)]
    for hc in range(HC):
        wf = wof[hc % 2]
        P.dma("gpsimd", wf[:], d_wo[hc], reads=(wf.name,), writes=(wf.name,))
        P.op("vector", lambda e, hc=hc, wf=wf: e.tensor_copy(
            out=woall[:, hc * DC * 128:(hc + 1) * DC * 128], in_=wf[:]),
            reads=(wf.name,), writes=("woall",))
    M.free("wof1", "wof0", P=P, keys=("wof0", "wof1"))
    gtc = [M.sb(f"gtc{i}", [128, DC * TS], BF16) for i in range(2)]
    x1r = [M.sb(f"x1r{i}", [128, TS], BF16) for i in range(2)]
    pso = [M.ps(f"pso{i}", [128, TS], F32) for i in range(2)]
    for tc in range(TCN):
        gb = gtc[tc % 2]
        for dc in range(DC):
            P.dma("gpsimd", gb[:, dc * TS:(dc + 1) * TS],
                  s_xc.ap()[:, dc * T + tc * TS: dc * T + (tc + 1) * TS],
                  reads=("s_xc",), writes=(gb.name,))
        for hc in range(HC):
            x1_ = x1r[hc % 2]
            P.dma("sync", x1_[:],
                  s_x1.ap()[:, hc * T + tc * TS: hc * T + (tc + 1) * TS],
                  reads=("s_x1",), writes=(x1_.name,))
            po_ = pso[hc % 2]

            def mmo(e, hc=hc, po_=po_, gb=gb):
                for dc in range(DC):
                    i = e.matmul(po_[:], woall[:, hc * DC * 128 + dc * 128:
                                               hc * DC * 128 + (dc + 1) * 128],
                                 gb[:, dc * TS:(dc + 1) * TS],
                                 start=(dc == 0), stop=(dc == DC - 1))
                return i
            P.op("tensor", mmo, reads=("woall", gb.name), writes=(po_.name,))
            P.op("vector", lambda e, hc=hc, tc=tc, x1_=x1_, po_=po_: e.scalar_tensor_tensor(
                out=x2[:, hc * T + tc * TS: hc * T + (tc + 1) * TS], in0=po_[:],
                scalar=col(c_bo, hc), in1=x1_[:], op0=ALU.add, op1=ALU.add),
                reads=(po_.name, "c_bo", x1_.name), writes=("x2",))
        ln_tc(tc, x2, "x2", zb1, "zb1", True)
    M.free("woall", "gtc0", "gtc1", "x1r0", "x1r1", "pso0", "pso1", P=P,
           keys=("woall", "gtc0", "gtc1", "x1r0", "x1r1", "pso0", "pso1"))
    if "x2" in taps:
        P.dma("sync", d_taps["x2"], x2[:], reads=("x2",), writes=("tapx2",))

    ffn(1, x2, "x2", None, None, final=True, zb_pre=zb1)

    P.barrier_wait("sync", ("y_out",))

    with nc.Block() as block:
        P.finalize(block)
    es.close()
    return nc


_NC_CACHE = {}


def _get_nc(taps=()):
    key = tuple(taps)
    if key not in _NC_CACHE:
        _NC_CACHE[key] = build_program(taps)
    return _NC_CACHE[key]


def _stage_inputs(inputs, taps=()):
    f32 = np.float32
    bf = ml_dtypes.bfloat16
    w_in = np.asarray(inputs["L0_attn_in_proj_w"], f32)
    w_out = np.asarray(inputs["L0_attn_out_proj_w"], f32)
    com = {
        "w_in_t": np.ascontiguousarray(
            w_in.reshape(HC, 128, 2 * DC, 128).transpose(2, 1, 0, 3)
            .reshape(2 * DC, 128, HC * 128)),
        "w_out_t": np.ascontiguousarray(
            w_out.reshape(DC, 128, HC, 128).transpose(2, 1, 0, 3)
            .reshape(HC, 128, DC * 128)),
        "w_gate": np.ascontiguousarray(np.asarray(inputs["L0_ffn_w_gate"], f32)),
        "w_up": np.ascontiguousarray(np.asarray(inputs["L0_ffn_w_up"], f32)),
        "w_down": np.ascontiguousarray(np.asarray(inputs["L0_ffn_w_down"], f32)),
        "x_proj_w": np.ascontiguousarray(np.asarray(inputs["L0_attn_x_proj_w"], f32)),
        "dt_proj_w": np.ascontiguousarray(np.asarray(inputs["L0_attn_dt_proj_w"], f32)),
        "b_in_t": np.ascontiguousarray(
            np.asarray(inputs["L0_attn_in_proj_b"], f32).reshape(2 * DC, 128).T),
        "b_out_t": np.ascontiguousarray(
            np.asarray(inputs["L0_attn_out_proj_b"], f32).reshape(HC, 128).T),
        "x_proj_b3": np.ascontiguousarray(
            np.asarray(inputs["L0_attn_x_proj_b"], f32).reshape(3, NST).T),
        "dt_b_t": np.ascontiguousarray(
            np.asarray(inputs["L0_attn_dt_proj_b"], f32).reshape(DC, 128).T),
        "conv_w_t": np.ascontiguousarray(
            np.asarray(inputs["L0_attn_conv_w"], f32).reshape(4, DC, 128)
            .transpose(2, 1, 0).reshape(128, DC * 4)),
        "conv_b_t": np.ascontiguousarray(
            np.asarray(inputs["L0_attn_conv_b"], f32).reshape(DC, 128).T),
        "a_log_t": np.ascontiguousarray(
            np.asarray(inputs["L0_attn_A_log"], f32).reshape(DC, 128, NST)
            .transpose(1, 0, 2).reshape(128, DC * NST)),
        "d_skip_t": np.ascontiguousarray(
            np.asarray(inputs["L0_attn_D_skip"], f32).reshape(DC, 128).T),
        "clip_x": np.asarray(inputs["L0_ffn_clip_x"], f32).reshape(1, 1),
        "clip_down": np.asarray(inputs["L0_ffn_clip_down"], f32).reshape(1, 1),
        "eye128": np.eye(128, dtype=bf),
        "ones1x128": np.ones((1, 128), dtype=bf),
        "ones1f": np.ones((1, 128), dtype=f32),
    }
    x = np.asarray(inputs["L0_attn_x"], f32)
    xs = x.reshape(NCORES, T, H)
    in_maps = []
    for c in range(NCORES):
        p = c % CHAIN
        im = dict(com)
        im["x_t"] = np.ascontiguousarray(xs[c].T)
        im["rank"] = np.array([[c]], dtype=np.int32)
        msk = np.zeros((128, 3), f32)
        for k in range(3):
            if p >= k + 1:
                msk[:, k] = 1.0
        im["chain_masks"] = msk
        bsel = np.zeros((128, 8), f32)
        if p >= 1:
            bsel[:, c - 1] = 1.0
        im["bnd_sel"] = bsel
        in_maps.append(im)
    return in_maps


def kernel(**inputs):
    taps = tuple(_TAPS)
    nc = _get_nc(taps)
    in_maps = _stage_inputs(inputs, taps)
    res = run_bass_kernel_spmd(nc, in_maps, core_ids=list(range(NCORES)))
    out = np.empty((B, S, H), np.float32)
    outs = out.reshape(NCORES, T, H)
    for c in range(NCORES):
        outs[c] = res.results[c]["y"].T
    if taps:
        kernel._tap_results = [{k: r[f"tap_{k}"] for k in taps} for r in res.results]
    return out
